# version 09
# baseline (speedup 1.0000x reference)
import numpy as np
import ml_dtypes

import concourse.bass as bass
import concourse.mybir as mybir
from concourse.bass_utils import run_bass_kernel_spmd

F32 = mybir.dt.float32
BF16 = mybir.dt.bfloat16
AF = mybir.ActivationFunctionType
ALU = mybir.AluOpType
AX = mybir.AxisListType
BF = ml_dtypes.bfloat16

D = 1024
T = 2048
NG = 8
NEG = np.float32(-3.0e38)


def _pm(w, mchunk):
    K, M = w.shape
    assert K % 128 == 0 and M % mchunk == 0, (K, M, mchunk)
    nk, nm = K // 128, M // mchunk
    r = w.reshape(nk, 128, nm, mchunk).transpose(1, 2, 0, 3)
    return np.ascontiguousarray(r.reshape(128, nm * nk * mchunk))


def _padc(w, cols):
    out = np.zeros((w.shape[0], cols), np.float32)
    out[:, : w.shape[1]] = w
    return out


def _padr(w, rows):
    out = np.zeros((rows, w.shape[1]), np.float32)
    out[: w.shape[0], :] = w
    return out


def _cols(v):
    assert v.shape[0] % 128 == 0
    return np.ascontiguousarray(v.reshape(-1, 128).T.astype(np.float32))


def _rotary_tables():
    DH, W, n = 64, 256, T
    half = np.arange(0, DH, 2, dtype=np.float32)
    inv_freq = (1.0 / (10000.0 ** (half / DH))).astype(np.float32)
    t = np.arange(n, dtype=np.float32)
    fr = t[:, None] * inv_freq[None, :]
    freqs = np.concatenate([fr, fr], -1)
    base = ((half + 0.4 * DH) / (1.4 * DH)).astype(np.float32)
    power = ((t - n // 2) / (W // 2)).astype(np.float32)
    sc = base[None, :] ** power[:, None]
    scale = np.concatenate([sc, sc], -1).astype(np.float32)
    cos, sin = np.cos(freqs), np.sin(freqs)
    ct = (cos * scale).T
    st = (sin * scale).T
    cti = (cos / scale).T
    sti = (sin / scale).T
    f = lambda a: np.ascontiguousarray(np.tile(a, (2, 1)).astype(BF))
    return f(ct), f(st), f(cti), f(sti)


def _host_constants():
    c = {}
    c["ident"] = np.eye(128, dtype=BF)
    R = np.zeros((128, 128), np.float32)
    for b in (0, 64):
        for dd in range(32):
            R[b + dd + 32, b + dd] = -1.0
            R[b + dd, b + dd + 32] = 1.0
    c["rot"] = R.astype(BF)
    ct, st, cti, sti = _rotary_tables()
    c["tab_ct"], c["tab_st"], c["tab_cti"], c["tab_sti"] = ct, st, cti, sti
    seg = np.zeros((128, 128), np.float32)
    for m in range(8):
        seg[0:64, m * 16 + 2 * m] = 1.0
        seg[64:128, m * 16 + 2 * m + 1] = 1.0
    c["seg"] = seg.astype(BF)
    e = np.zeros((16, 128 * 8), np.float32)
    for g in range(8):
        e[2 * g, g * 128 + 0:g * 128 + 64] = 1.0
        e[2 * g + 1, g * 128 + 64:g * 128 + 128] = 1.0
    c["exp16"] = e.astype(BF)
    r = np.arange(128)[:, None]
    j1 = np.arange(128)[None, :]
    tri_ge = np.where(j1 >= r, 0.0, NEG)
    tri_le = np.where(j1 <= r, 0.0, NEG)
    c["cmask"] = np.concatenate(
        [tri_ge, np.zeros((128, 128), np.float32), tri_le], 1).astype(BF)
    c["w0m0"] = tri_le.astype(BF)
    c["w0m1"] = np.concatenate(
        [np.zeros((128, 128), np.float32), tri_le], 1).astype(BF)
    mwp = np.full((128, 256), NEG, np.float32)
    mwp[0:64, 0:192] = 0.0
    mwp[64:128, 64:256] = 0.0
    c["mwpair"] = mwp.astype(BF)
    c["ones_f"] = np.ones((128, 128), np.float32)
    c["ones_b"] = np.ones((128, 128), dtype=BF)
    c["onescol_b"] = np.ones((128, 1), dtype=BF)
    return c


def prep_inputs(I):
    d = dict(_host_constants())
    d["a_wq"] = _pm(I["L0_attn_Wq"], 128)
    d["a_wk"] = _pm(I["L0_attn_Wk"], 128)
    d["a_wv"] = _pm(I["L0_attn_Wv"], 512)
    d["a_wo"] = _pm(I["L0_attn_Wo"], 128)
    d["a_bq"] = _cols(I["L0_attn_bq"])
    d["a_bk"] = _cols(I["L0_attn_bk"])
    d["a_bv"] = _cols(I["L0_attn_bv"])
    d["a_bo"] = _cols(I["L0_attn_bo"])
    qkv = I["L0_ffn_Wqkv"]
    for i in range(6):
        d[f"f_wq{i}"] = _pm(qkv[i][:, 0:1024], 128)
        d[f"f_wk{i}"] = _pm(qkv[i][:, 1024:2048], 128)
        d[f"f_wv{i}"] = _pm(qkv[i][:, 2048:3072], 512)
        d[f"f_wo{i}"] = _pm(I["L0_ffn_Wo"][i], 128)
        d[f"f_w1a{i}"] = _pm(_padc(I["L0_ffn_W1"][i][:, 0:2730], 2816), 128)
        d[f"f_w1g{i}"] = _pm(_padc(I["L0_ffn_W1"][i][:, 2730:5460], 2816), 128)
        d[f"f_w2{i}"] = _pm(_padr(I["L0_ffn_W2"][i], 2816), 128)
    d["f_qs"] = np.ascontiguousarray(
        np.tile(I["L0_ffn_q_scale"], (1, 2)).T.astype(np.float32))
    d["f_ks"] = np.ascontiguousarray(
        np.tile(I["L0_ffn_k_scale"], (1, 2)).T.astype(np.float32))
    d["f_g"] = np.concatenate([_cols(I["L0_ffn_ln_g"][i]) for i in range(6)], 1)
    d["f_b"] = np.concatenate([_cols(I["L0_ffn_ln_b"][i]) for i in range(6)], 1)
    d["f_b1a"] = np.concatenate(
        [_cols(_padc(I["L0_ffn_b1"][i][None, 0:2730], 2816)[0]) for i in range(6)], 1)
    d["f_b1g"] = np.concatenate(
        [_cols(_padc(I["L0_ffn_b1"][i][None, 2730:5460], 2816)[0]) for i in range(6)], 1)
    d["f_b2"] = np.concatenate([_cols(I["L0_ffn_b2"][i]) for i in range(6)], 1)
    d["b_wq"] = _pm(np.ascontiguousarray(I["L1_attn_Wq"].T), 128)
    d["b_wk"] = _pm(np.ascontiguousarray(I["L1_attn_Wk"].T), 512)
    d["b_wv"] = _pm(np.ascontiguousarray(I["L1_attn_Wv"].T), 512)
    d["b_wo"] = _pm(np.ascontiguousarray(I["L1_attn_Wo"].T), 128)
    d["m_gate"] = _pm(np.ascontiguousarray(I["L1_ffn_gate_W"].T), 8)
    d["m_gbc"] = np.ascontiguousarray(
        np.broadcast_to(I["L1_ffn_gate_b"], (128, 8)).astype(np.float32))
    d["m_wi"] = _pm(np.ascontiguousarray(I["L1_ffn_wi_W"].T), 128)
    d["m_wo"] = _pm(np.ascontiguousarray(I["L1_ffn_wo_W"].T), 128)
    d["m_wib"] = _cols(I["L1_ffn_wi_b"])
    d["m_wob"] = _cols(I["L1_ffn_wo_b"])
    d["m_la"] = _pm(np.ascontiguousarray(I["L1_ffn_lora_A"].reshape(32, 1024).T), 32)
    d["m_lb"] = np.ascontiguousarray(
        I["L1_ffn_lora_B"].transpose(0, 2, 1).reshape(32, 4096).astype(np.float32))
    x = I["L0_attn_x"]
    xt = np.ascontiguousarray(x.transpose(0, 2, 1))
    xs = [np.ascontiguousarray(xt[2 * c:2 * c + 2]) for c in range(8)]
    return d, xs



ENGS = ("te", "ve", "sc", "gp", "sy")
SEMS = ("s_te", "s_ve", "s_sc", "s_gp", "s_sy", "d_gp", "d_sy", "d_sc",
        "d_w0", "d_w1")


class K:
    def __init__(self, nc):
        self.nc = nc
        self.streams = {e: [] for e in ENGS}
        self.cnt = {s: 0 for s in SEMS}
        self.lastw = {}
        self.semh = {}
        self.scope = None

    def set_scope(self, name):
        self.scope = name

    def emit(self, eng, fn):
        fn._scope = self.scope
        self.streams[eng].append(fn)

    def wait(self, eng, sem, val):
        if val <= 0:
            return
        assert val <= self.cnt[sem], (eng, sem, val, self.cnt[sem])
        if self.lastw.get((eng, sem), 0) >= val:
            return
        self.lastw[(eng, sem)] = val

        def f(e, s=sem, v=val):
            e.wait_ge(self.semh[s], v)
        self.emit(eng, f)

    CHAIN = {"ve": "s_ve", "sc": "s_sc", "gp": "s_gp"}

    def op(self, eng, fname, *args, waits=(), inc=None, **kw):
        chain = self.CHAIN.get(eng) if fname != "dma_start" else None
        for s, v in waits:
            self.wait(eng, s, v)
        if chain is not None:
            self.wait(eng, chain, self.cnt[chain])
            if inc is None:
                inc = (chain, 1)
        tok = None
        if inc is not None:
            if isinstance(inc, str):
                inc = (inc, 1)
            assert chain is None or inc[0] == chain, (eng, fname, inc)
            self.cnt[inc[0]] += inc[1]
            tok = self.cnt[inc[0]]
            inc = (inc[0], inc[1], tok)

        def f(e, fname=fname, args=args, kw=kw, inc=inc):
            ins = getattr(e, fname)(*args, **kw)
            if inc is not None:
                ins.then_inc(self.semh[inc[0]], inc[1])
        self.emit(eng, f)
        return tok

    def dma(self, q, out, in_, waits=(), sem=None):
        if sem is None:
            sem = {"gp": "d_gp", "sy": "d_sy", "sc": "d_sc"}[q]
        self.op(q, "dma_start", waits=waits, inc=(sem, 16), out=out, in_=in_)
        return sem, self.cnt[sem]

    def barrier(self, engs=ENGS):
        for e in engs:
            if e == "gp":
                self.wait("gp", "d_gp", self.cnt["d_gp"])
                self.wait("gp", "d_w0", self.cnt["d_w0"])
                self.wait("gp", "d_w1", self.cnt["d_w1"])
            if e == "sy":
                self.wait("sy", "d_sy", self.cnt["d_sy"])
            if e == "sc":
                self.wait("sc", "d_sc", self.cnt["d_sc"])
        tgt = {}
        for e in engs:
            tgt[e] = self.op(e, "drain", inc="s_" + e)
        for e in engs:
            for o in engs:
                if o != e:
                    self.wait(e, "s_" + o, tgt[o])

    def build_block(self):
        nc = self.nc
        import contextlib
        stack = contextlib.ExitStack()
        for s in SEMS:
            self.semh[s] = stack.enter_context(nc.semaphore(s))
        def run_stream(eng_obj, fns):
            cur = None
            cur_id = None
            for f in fns:
                sc = getattr(f, "_scope", None)
                if sc != cur:
                    if cur is not None:
                        nc.leave_named_scope(cur, cur_id, False)
                    if sc is not None:
                        cur_id, _ = nc.enter_named_scope(sc, False)
                    cur = sc
                f(eng_obj)
            if cur is not None:
                nc.leave_named_scope(cur, cur_id, False)

        with stack:
            with nc.Block() as block:
                st = self.streams

                @block.tensor
                def _(t):
                    run_stream(t, st["te"])

                @block.vector
                def _(v):
                    run_stream(v, st["ve"])

                @block.scalar
                def _(s):
                    run_stream(s, st["sc"])

                @block.gpsimd
                def _(g):
                    run_stream(g, st["gp"])

                @block.sync
                def _(s):
                    run_stream(s, st["sy"])
        return nc



SHARED_SPECS = {}


def _declare_dram(nc, shared_shapes):
    t = {}
    for name, (shape, dt) in shared_shapes.items():
        t[name] = nc.dram_tensor(name, list(shape), dt, kind="ExternalInput").ap()
    return t


class Buffers:
    pass


def alloc_buffers(nc):
    B = Buffers()
    a = nc.alloc_sbuf_tensor
    B.y = [a(f"y{g}", [128, 2049], F32).ap() for g in range(NG)]
    B.poolA = a("poolA", [128, 16384], BF16).ap()
    B.poolB = a("poolB", [128, 16384], BF16).ap()
    B.poolC = a("poolC", [128, 16384], BF16).ap()
    B.poolF = a("poolF", [128, 4096], F32).ap()
    B.tabs = a("tabs", [128, 4096], BF16).ap()
    B.scr = a("scr", [128, 2816], F32).ap()
    B.col = a("colscr", [128, 64], F32).ap()
    B.rows16 = a("rows16", [16, 2048], BF16).ap()
    B.ident = a("c_ident", [128, 128], BF16).ap()
    B.rot = a("c_rot", [128, 128], BF16).ap()
    B.seg = a("c_seg", [128, 128], BF16).ap()
    B.exp16 = a("c_exp16", [16, 1024], BF16).ap()
    B.ones_f = a("c_ones_f", [128, 128], F32).ap()
    B.ones_b = a("c_ones_b", [128, 128], BF16).ap()
    B.onescol_b = a("c_onescol_b", [128, 1], BF16).ap()
    B.cmask = a("c_cmask", [128, 384], BF16).ap()
    B.w0m0 = a("c_w0m0", [128, 128], BF16).ap()
    B.w0m1 = a("c_w0m1", [128, 256], BF16).ap()
    B.mwp = a("c_mwp", [128, 256], BF16).ap()
    B.v_abq = a("v_abq", [128, 8], F32).ap()
    B.v_abk = a("v_abk", [128, 8], F32).ap()
    B.v_abv = a("v_abv", [128, 8], F32).ap()
    B.v_abo = a("v_abo", [128, 8], F32).ap()
    B.v_abqs = a("v_abqs", [128, 8], F32).ap()
    B.v_abks = a("v_abks", [128, 8], F32).ap()
    B.v_qs = a("v_qs", [128, 6], F32).ap()
    B.v_ks = a("v_ks", [128, 6], F32).ap()
    B.v_g = a("v_g", [128, 48], F32).ap()
    B.v_b = a("v_b", [128, 48], F32).ap()
    B.v_b1a = a("v_b1a", [128, 22], F32).ap()
    B.v_b1g = a("v_b1g", [128, 22], F32).ap()
    B.v_b2 = a("v_b2", [128, 48], F32).ap()
    B.v_wib = a("v_wib", [128, 32], F32).ap()
    B.v_wob = a("v_wob", [128, 8], F32).ap()
    B.v_gbc = a("v_gbc", [128, 8], F32).ap()
    B.eps5 = a("v_eps5", [128, 1], F32).ap()
    B.PSA = nc.alloc_psum_tensor("PSA", [128, 2048], F32).ap()
    B.PSB = nc.alloc_psum_tensor("PSB", [128, 1024], F32).ap()
    B.PSC = nc.alloc_psum_tensor("PSC", [128, 512], F32).ap()
    B.PST = nc.alloc_psum_tensor("PST", [128, 1024], BF16).ap()
    return B


def load_statics(K, B, t):
    pairs = [
        (B.ident, "ident"), (B.rot, "rot"), (B.seg, "seg"), (B.exp16, "exp16"),
        (B.ones_f, "ones_f"), (B.ones_b, "ones_b"), (B.onescol_b, "onescol_b"),
        (B.cmask, "cmask"), (B.w0m0, "w0m0"), (B.w0m1, "w0m1"),
        (B.mwp, "mwpair"),
        (B.v_abq, "a_bq"), (B.v_abk, "a_bk"), (B.v_abv, "a_bv"),
        (B.v_abo, "a_bo"),
        (B.v_qs, "f_qs"), (B.v_ks, "f_ks"), (B.v_g, "f_g"), (B.v_b, "f_b"),
        (B.v_b2, "f_b2"),
        (B.v_wib, "m_wib"), (B.v_wob, "m_wob"), (B.v_gbc, "m_gbc"),
    ]
    for dst, name in pairs:
        K.dma("sy", dst, t[name])
    tok = K.cnt["d_sy"]
    K.op("ve", "tensor_scalar_mul", B.v_abqs, B.v_abq, 0.125,
         waits=[("d_sy", tok)])
    K.op("ve", "tensor_scalar_mul", B.v_abks, B.v_abk, 0.125, inc="s_ve")
    K.op("ve", "memset", B.eps5, 1e-5)
    return tok


def proj_fmajor(K, B, wdram, nm, rhs_fn, evac_fn, t_tiles, wbuf0, wbuf1,
                nk=8, mc=128):
    wbufs = [wbuf0, wbuf1]
    wtok = [("d_w0", 0), ("d_w1", 0)]
    te0 = K.cnt["s_te"]
    wfree = [te0, te0]
    banks = [(B.PSA[:, 512 * i:512 * i + 512]) for i in range(4)]
    bfree = [K.cnt["s_ve"]] * 4
    bsem = ["s_ve"] * 4
    sc0 = K.cnt["s_sc"]
    bi = 0
    for m in range(nm):
        wb = m % 2
        K.wait("gp", "s_te", wfree[wb])
        wtok[wb] = K.dma("gp", wbufs[wb],
                         wdram[:, m * nk * mc:(m + 1) * nk * mc],
                         sem=f"d_w{wb}")
        for tt, tw in t_tiles:
            ps = banks[bi % 4][:, 0:tw]
            K.wait("te", *wtok[wb])
            K.wait("te", bsem[bi % 4], bfree[bi % 4])
            if bi < 4:
                K.wait("te", "s_sc", sc0)
            for kc in range(nk):
                K.op("te", "matmul", ps, wbufs[wb][:, kc * mc:(kc + 1) * mc],
                     rhs_fn(kc, tt), start=(kc == 0), stop=(kc == nk - 1),
                     inc=("s_te" if kc == nk - 1 else None))
            tetok = K.cnt["s_te"]
            if tt == t_tiles[-1][0]:
                wfree[wb] = tetok
            sem, ftok = evac_fn(m, tt, ps, tetok)
            bsem[bi % 4], bfree[bi % 4] = sem, ftok
            bi += 1


def stream_x_tile(K, B, xsrc, s, tt, tw=512):
    stg = [B.poolF[:, g * 512:g * 512 + tw] for g in range(NG)]
    for g in range(NG):
        K.dma("sy", stg[g], xsrc[s, g * 128:(g + 1) * 128, tt * 512:tt * 512 + tw])
    return stg, K.cnt["d_sy"]


def emit_ln_stats(K, B, srcs, tw, sq_bf, psA, psB, waits=()):
    is_bf = srcs[0].dtype == BF16
    for g in range(NG):
        K.op("te", "matmul", psA[:, 0:tw],
             B.ones_b if is_bf else B.ones_f, srcs[g],
             start=(g == 0), stop=(g == NG - 1),
             waits=(list(waits) if g == 0 else []),
             inc=("s_te" if g == NG - 1 else None))
    for g in range(NG):
        w = list(waits) if g == 0 else [("s_te", K.cnt["s_te"])]
        sctok = K.op("sc", "activation", sq_bf[:, 0:tw], srcs[g], AF.Square,
                     waits=w, inc="s_sc")
        K.op("te", "matmul", psB[:, 0:tw], B.ones_b, sq_bf[:, 0:tw],
             start=(g == 0), stop=(g == NG - 1), waits=[("s_sc", sctok)],
             inc="s_te")
    return K.cnt["s_te"]


def emit_ln_factors(K, B, psA, psB, tw, tetok, eps=1e-5):
    t1 = B.scr[:, 0:tw]
    t2 = B.scr[:, 512:512 + tw]
    K.op("ve", "tensor_scalar_mul", t1, psA[:, 0:tw], 1.0 / D,
         waits=[("s_te", tetok)])
    vtok = K.op("ve", "tensor_scalar_mul", t2, psB[:, 0:tw], 1.0 / D,
                inc="s_ve")
    stok = K.op("sc", "activation", psA[:, 0:tw], t1, AF.Square,
                waits=[("s_ve", vtok)], inc="s_sc")
    vtok = K.op("ve", "tensor_sub", t2, t2, psA[:, 0:tw],
                waits=[("s_sc", stok)], inc="s_ve")
    stok = K.op("sc", "activation", psB[:, 0:tw], t2, AF.Sqrt,
                bias=B.eps5[:, 0:1],
                waits=[("s_ve", vtok)], inc="s_sc")
    K.op("ve", "reciprocal", t2, psB[:, 0:tw],
         waits=[("s_sc", stok)])
    vtok = K.op("ve", "tensor_mul", t1, t1, t2, inc="s_ve")
    return t2, t1, vtok



def emit_preln(K, B, xsrc, s, dst_fn, out_f32=None, bank_guard=None):
    sq = B.scr.bitcast(BF16)[:, 2048:2560]
    psA = B.PSA[:, 0:512]
    psB = B.PSA[:, 512:1024]
    vfree = 0
    for tt in range(4):
        stg = [B.poolF[:, g * 512:(g + 1) * 512] for g in range(NG)]
        for g in range(NG):
            K.wait("sy", "s_ve", vfree)
            if bank_guard and tt == 0 and g == 0:
                K.wait("sy", *bank_guard)
            K.dma("sy", stg[g], xsrc[s, g * 128:(g + 1) * 128,
                                     tt * 512:(tt + 1) * 512])
        dtok = K.cnt["d_sy"]
        tetok = emit_ln_stats(K, B, stg, 512, sq, psA, psB,
                              waits=[("d_sy", dtok), ("s_ve", vfree)])
        a, c, vtok = emit_ln_factors(K, B, psA, psB, 512, tetok)
        for g in range(NG):
            K.op("ve", "tensor_mul", stg[g], stg[g], a)
            dst = dst_fn(g, tt) if out_f32 is None else out_f32(g, tt)
            vfree = K.op("ve", "tensor_sub", dst, stg[g], c, inc="s_ve")


def emit_postln_residual(K, B, fsrc_fn, xsrc, xdst, s, waits=(), tts=(0, 1, 2, 3)):
    sq = B.scr.bitcast(BF16)[:, 2048:2560]
    psA = B.PSA[:, 0:512]
    psB = B.PSA[:, 512:1024]
    tmpf = B.scr[:, 1280:1792]
    vfree = 0
    for tt in tts:
        stg = [B.poolF[:, g * 512:(g + 1) * 512] for g in range(NG)]
        K.wait("sy", "d_sy", K.cnt["d_sy"])
        for g in range(NG):
            K.wait("sy", "s_ve", vfree)
            K.dma("sy", stg[g], xsrc[s, g * 128:(g + 1) * 128,
                                     tt * 512:(tt + 1) * 512])
        dtok = K.cnt["d_sy"]
        fsrcs = [fsrc_fn(g, tt) for g in range(NG)]
        tetok = emit_ln_stats(K, B, fsrcs, 512, sq, psA, psB,
                              waits=list(waits) + [("s_ve", vfree)])
        a, c, vtok = emit_ln_factors(K, B, psA, psB, 512, tetok)
        for g in range(NG):
            K.op("ve", "tensor_mul", tmpf, fsrcs[g], a,
                 waits=[("d_sy", dtok)] if g == 0 else [])
            K.op("ve", "tensor_sub", tmpf, tmpf, c)
            vfree = K.op("ve", "tensor_add", stg[g], stg[g], tmpf, inc="s_ve")
            K.dma("sy", xdst[s, g * 128:(g + 1) * 128,
                             tt * 512:(tt + 1) * 512], stg[g],
                  waits=[("s_ve", vfree)])
    return K.cnt["d_sy"]





def proj_tmajor(K, B, wdram, lhs_fn, evac_fn, nk=8, tcs_all=None,
                ve_guard=None, co_fn=None):
    wv = [B.scr.bitcast(BF16)[:, 3584:4096], B.scr.bitcast(BF16)[:, 4096:4608]]
    wvtok = [("d_w0", 0), ("d_w1", 0)]
    wvfree = [K.cnt["s_te"]] * 2
    sc0 = K.cnt["s_sc"]
    ve0 = K.cnt["s_ve"] if ve_guard is None else ve_guard
    banks6 = [B.PSA[:, 512 * i:512 * (i + 1)] for i in range(4)] + \
             [B.PSB[:, 0:512], B.PSB[:, 512:1024]]
    bfree = [("s_sc", sc0)] * 6
    K.wait("te", "s_ve", ve0)
    K.wait("gp", "s_ve", ve0)
    K.wait("gp", "s_sc", sc0)
    if tcs_all is None:
        tcs_all = list(range(16))
    groups = [tcs_all[i:i + 6] for i in range(0, len(tcs_all), 6)]
    for tcs in groups:
        for sl in range(2):
            gtok = [0] * len(tcs)
            for kc in range(nk):
                wbi = kc % 2
                K.wait("gp", "s_te", wvfree[wbi])
                wvtok[wbi] = K.dma("gp", wv[wbi],
                                   wdram[:, (sl * nk + kc) * 512:
                                         (sl * nk + kc + 1) * 512],
                                   sem=f"d_w{wbi}")
                for j, tc in enumerate(tcs):
                    ps = banks6[j]
                    K.wait("te", *wvtok[wbi])
                    if kc == 0:
                        K.wait("te", *bfree[j])
                    last = (j == len(tcs) - 1) or kc == nk - 1
                    K.op("te", "matmul", ps,
                         lhs_fn(kc)[:, tc * 128:(tc + 1) * 128], wv[wbi],
                         start=(kc == 0), stop=(kc == nk - 1),
                         inc=("s_te" if last else None))
                    if kc == nk - 1:
                        gtok[j] = K.cnt["s_te"]
                wvfree[wbi] = K.cnt["s_te"]
            for j, tc in enumerate(tcs):
                bfree[j] = evac_fn(tc, sl, banks6[j], gtok[j])
                if co_fn is not None:
                    co_fn()


def elu_evac(K, B, dst, ps, tetok):
    t1 = B.scr[:, 0:512][:, 0:dst.shape[1]]
    t2 = B.scr[:, 512:1024][:, 0:dst.shape[1]]
    te = B.scr[:, 1024:1536][:, 0:dst.shape[1]]
    vtok = K.op("ve", "tensor_scalar_min", t1, ps, 0.0,
                waits=[("s_te", tetok)])
    stok = K.op("sc", "activation", te, t1, AF.Exp, waits=[("s_ve", vtok)],
                inc="s_sc")
    vtok2 = K.op("ve", "tensor_scalar_max", t2, ps, 0.0, inc="s_ve")
    K.op("ve", "tensor_add", dst, te, t2, waits=[("s_sc", stok)])
    return "s_ve", vtok2


def emit_l0_attn(K, B, t, s, xsrc, xdst):
    base = K.scope
    nc = K.nc
    ybf = [B.y[g].bitcast(BF16) for g in range(NG)]
    xln = lambda g, tt: ybf[g][:, tt * 512:(tt + 1) * 512]
    xln_g = lambda g: ybf[g][:, 0:2048]
    ao2 = lambda g, tt: ybf[g][:, 2048 + tt * 512:2048 + (tt + 1) * 512]
    qb = [B.poolA[:, g * 2048:(g + 1) * 2048] for g in range(NG)]
    kb = [B.poolB[:, g * 2048:(g + 1) * 2048] for g in range(NG)]
    vt = [B.poolC[:, c * 1024:(c + 1) * 1024] for c in range(16)]
    wq = B.scr.bitcast(BF16)[:, 0:1024]
    wq2 = B.scr.bitcast(BF16)[:, 1024:2048]
    t_tiles = [(i, 512) for i in range(4)]

    K.set_scope(f"{base}.preln")
    emit_preln(K, B, xsrc, s, xln)
    K.barrier()
    K.set_scope(f"{base}.proj")

    def ev_q(buf, bias, scale):
        def f(m, tt, ps, tetok):
            tok = K.op("sc", "activation", buf[m][:, tt * 512:(tt + 1) * 512],
                       ps, AF.Identity, scale=scale, bias=bias[:, m:m + 1],
                       waits=[("s_te", tetok)], inc="s_sc")
            return "s_sc", tok
        return f
    rhs = lambda kc, tt: xln(kc, tt)
    proj_fmajor(K, B, t["a_wq"], 8, rhs, ev_q(qb, B.v_abqs, 0.125), t_tiles,
                wq, wq2)
    proj_fmajor(K, B, t["a_wk"], 8, rhs, ev_q(kb, B.v_abk, 1.0), t_tiles,
                wq, wq2)

    def ev_v(tc, sl, ps, gtok):
        return "s_sc", K.op("sc", "activation", vt[tc][:, sl * 512:(sl + 1) * 512],
                            ps, AF.Copy, waits=[("s_te", gtok)], inc="s_sc")
    proj_tmajor(K, B, t["a_wv"], xln_g, ev_v)
    K.set_scope(f"{base}.attn")
    K.barrier()
    return qb, kb, vt, xln_g, ao2


def emit_l0_attn_attention(K, B, qb, kb, vt):
    Pb = B.poolF.bitcast(BF16)
    C = B.col
    te0, ve0, sc0 = K.cnt["s_te"], K.cnt["s_ve"], K.cnt["s_sc"]

    units0 = [(g, h) for g in range(NG) for h in range(2)]
    exp_tok = [sc0] * len(units0)
    st_tok = [te0] * len(units0)
    mx_tok = [ve0] * len(units0)
    nm_tok = [ve0] * len(units0)
    tt_tok = [te0] * len(units0)
    ev_tok = [sc0] * len(units0)
    pv_tok = [te0] * len(units0)
    ot_tok = [sc0] * len(units0)

    def b0_front_te(u):
        g, h = units0[u]
        hb = 64 * h
        if u >= 1:
            K.wait("te", "s_sc", exp_tok[u - 1])
        tok = 0
        for c in range(4):
            tok = K.op("te", "matmul", B.PSA[0:64, 512 * c:512 * (c + 1)],
                       qb[g][hb:hb + 64, 0:64],
                       kb[g][hb:hb + 64, 512 * c:512 * (c + 1)],
                       start=True, stop=True,
                       inc=("s_te" if c == 3 else None))
        st_tok[u] = tok

    def b0_front_rest(u):
        g, h = units0[u]
        cb = 8 + 16 * (u % 2)
        mx_tok[u] = K.op("ve", "reduce_max", C[0:64, cb:cb + 1], B.PSA[0:64, 0:2048],
                         axis=AX.X, negate=True, waits=[("s_te", st_tok[u])],
                         inc="s_ve")
        pb0 = 4096 + 2048 * (u % 2)
        tok = 0
        for c in range(4):
            w_ = [("s_ve", mx_tok[u])]
            if u >= 2:
                w_.append(("s_te", tt_tok[u - 2]))
            tok = K.op("sc", "activation", Pb[0:64, pb0 + 512 * c:pb0 + 512 * (c + 1)],
                       B.PSA[0:64, 512 * c:512 * (c + 1)], AF.Exp,
                       bias=C[0:64, cb:cb + 1], accum_out=C[0:64, cb + 1 + c:cb + 2 + c],
                       waits=w_, inc="s_sc")
        exp_tok[u] = tok

    def b0_back(u):
        g, h = units0[u]
        hb = 64 * h
        vcol = g * 128 + hb
        cb = 8 + 16 * (u % 2)
        K.op("ve", "tensor_add", C[0:64, cb + 5:cb + 6], C[0:64, cb + 1:cb + 2], C[0:64, cb + 2:cb + 3],
             waits=[("s_sc", exp_tok[u])])
        K.op("ve", "tensor_add", C[0:64, cb + 6:cb + 7], C[0:64, cb + 3:cb + 4], C[0:64, cb + 4:cb + 5])
        K.op("ve", "tensor_add", C[0:64, cb + 5:cb + 6], C[0:64, cb + 5:cb + 6], C[0:64, cb + 6:cb + 7])
        pb0 = 4096 + 2048 * (u % 2)
        pt0 = 1024 * (u % 2)
        K.op("ve", "reciprocal", C[0:64, cb + 5:cb + 6], C[0:64, cb + 5:cb + 6])
        nm_tok[u] = K.op("ve", "tensor_scalar_mul", Pb[0:64, pb0:pb0 + 2048],
                         Pb[0:64, pb0:pb0 + 2048], C[0:64, cb + 5:cb + 6], inc="s_ve")
        K.wait("te", "s_sc", ev_tok[u - 1] if u >= 1 else sc0)
        tok = 0
        for c in range(16):
            tok = K.op("te", "transpose", B.PST[:, 64 * c:64 * (c + 1)],
                       Pb[0:64, pb0 + 128 * c:pb0 + 128 * (c + 1)],
                       B.ident[0:64, 0:64],
                       waits=[("s_ve", nm_tok[u])] if c == 0 else [],
                       inc=("s_te" if c == 15 else None))
        tt_tok[u] = tok
        w_ = [("s_te", tt_tok[u])]
        if u >= 1:
            w_.append(("s_te", pv_tok[u - 1]))
        ev_tok[u] = K.op("sc", "activation", Pb[:, pt0:pt0 + 1024], B.PST,
                         AF.Copy, waits=w_, inc="s_sc")
        if u >= 1:
            K.wait("te", "s_sc", ot_tok[u - 1])
        tok = 0
        for c in range(16):
            tok = K.op("te", "matmul", B.PSC[0:64, 0:64],
                       vt[c][:, vcol:vcol + 64],
                       Pb[:, pt0 + 64 * c:pt0 + 64 * (c + 1)],
                       start=(c == 0), stop=(c == 15),
                       waits=[("s_sc", ev_tok[u])] if c == 0 else [],
                       inc=("s_te" if c == 15 else None))
        pv_tok[u] = tok
        ot_tok[u] = K.op("sc", "activation", qb[g][hb:hb + 64, 0:64],
                         B.PSC[0:64, 0:64], AF.Identity,
                         bias=B.v_abv[hb:hb + 64, g:g + 1],
                         waits=[("s_te", pv_tok[u])], inc="s_sc")

    for u in range(len(units0)):
        b0_front_te(u)
        if u >= 1:
            b0_back(u - 1)
        b0_front_rest(u)
    b0_back(len(units0) - 1)
    K.barrier(("te", "ve", "sc"))

    te0, ve0, sc0 = K.cnt["s_te"], K.cnt["s_ve"], K.cnt["s_sc"]
    units = [(g, h, a) for g in range(NG) for h in range(2)
             for a in list(range(1, 30, 2)) + [31]]
    n = len(units)
    exp_tok = [sc0] * n
    stok = [te0] * n
    mxt = [ve0] * n
    nmt = [ve0] * n
    ttt = [te0] * n
    evt = [sc0] * n
    pvt = [te0] * n
    ott = [sc0] * n
    DEPTH = 4

    def geom(u):
        g, h, a = units[u]
        solo = (a == 31)
        kst = 64 * (a - 1)
        kw = 128 if solo else 256
        qw = 64 if solo else 128
        return g, h, 64 * h, a, solo, kst, kw, qw

    PSTs = [B.PST, B.PSB.bitcast(BF16)[:, 0:1024]]
    PSCs = [B.PSC[0:64, 0:256], B.PSB[0:64, 512:768]]

    def w_front_te(u):
        g, h, hb, a, solo, kst, kw, qw = geom(u)
        s = u % DEPTH
        ps = B.PSA[:, 512 * s:512 * s + kw]
        if u >= DEPTH:
            K.wait("te", "s_sc", exp_tok[u - DEPTH])
        stok[u] = K.op("te", "matmul", ps[0:qw, :],
                       qb[g][hb:hb + 64, 64 * a:64 * a + qw],
                       kb[g][hb:hb + 64, kst:kst + kw],
                       start=True, stop=solo,
                       inc=("s_te" if solo else None))
        if not solo:
            stok[u] = K.op("te", "matmul", ps, B.ident, B.mwp,
                           start=False, stop=True, inc="s_te")

    def w_front_rest(u):
        g, h, hb, a, solo, kst, kw, qw = geom(u)
        s = u % DEPTH
        ps = B.PSA[:, 512 * s:512 * s + kw]
        mxt[u] = K.op("ve", "reduce_max", C[0:qw, s:s + 1], ps[0:qw, :],
                      axis=AX.X, negate=True, waits=[("s_te", stok[u])],
                      inc="s_ve")
        Pp = Pb[0:qw, 1024 * s:1024 * s + kw]
        w_ = [("s_ve", mxt[u])]
        if u >= DEPTH:
            w_.append(("s_te", ttt[u - DEPTH]))
        exp_tok[u] = K.op("sc", "activation", Pp, ps[0:qw, :], AF.Exp,
                          bias=C[0:qw, s:s + 1], accum_out=C[0:qw, 4 + s:5 + s],
                          waits=w_, inc="s_sc")

    def w_mid(u):
        g, h, hb, a, solo, kst, kw, qw = geom(u)
        s = u % DEPTH
        Pp = Pb[0:qw, 1024 * s:1024 * s + kw]
        K.op("ve", "reciprocal", C[0:qw, 4 + s:5 + s], C[0:qw, 4 + s:5 + s],
             waits=[("s_sc", exp_tok[u])])
        nmt[u] = K.op("ve", "tensor_scalar_mul", Pp, Pp, C[0:qw, 4 + s:5 + s],
                      inc="s_ve")
        nb = kw // 128
        pst = PSTs[u % 2]
        K.wait("te", "s_sc", evt[u - 2] if u >= 2 else sc0)
        tok = 0
        for c in range(nb):
            tok = K.op("te", "transpose",
                       pst[:, 128 * c:128 * c + qw],
                       Pb[0:qw, 1024 * s + 128 * c:1024 * s + 128 * (c + 1)],
                       B.ident[0:qw, 0:qw],
                       waits=[("s_ve", nmt[u])] if c == 0 else [],
                       inc=("s_te" if c == nb - 1 else None))
        ttt[u] = tok
        w_ = [("s_te", ttt[u])]
        if u >= 2:
            w_.append(("s_te", pvt[u - 2]))
        wpt = 128 * (nb - 1) + qw
        PT = Pb[:, 4096 + 512 * (u % 2):4096 + 512 * (u % 2) + wpt]
        evt[u] = K.op("sc", "activation", PT, pst[:, 0:wpt], AF.Copy,
                      waits=w_, inc="s_sc")
        c0 = (a - 1) // 2
        vcol = g * 128 + hb
        psc = PSCs[u % 2]
        if u >= 2:
            K.wait("te", "s_sc", ott[u - 2])
        tok = 0
        for c in range(nb):
            tok = K.op("te", "matmul", psc[:, 0:qw],
                       vt[c0 + c][:, vcol:vcol + 64],
                       PT[:, 128 * c:128 * c + qw],
                       start=(c == 0), stop=(c == nb - 1),
                       waits=[("s_sc", evt[u])] if c == 0 else [],
                       inc=("s_te" if c == nb - 1 else None))
        pvt[u] = tok

    def w_tail(u):
        g, h, hb, a, solo, kst, kw, qw = geom(u)
        ott[u] = K.op("sc", "activation", qb[g][hb:hb + 64, 64 * a:64 * a + qw],
                      PSCs[u % 2][:, 0:qw], AF.Identity,
                      bias=B.v_abv[hb:hb + 64, g:g + 1],
                      waits=[("s_te", pvt[u])], inc="s_sc")

    for u in range(n):
        w_front_te(u)
        if u >= 1:
            w_mid(u - 1)
        if u >= 2:
            w_tail(u - 2)
        w_front_rest(u)
    w_mid(n - 1)
    w_tail(n - 2)
    w_tail(n - 1)


def emit_l0a(K, B, t, s, xsrc, xdst):
    base = K.scope
    qb, kb, vt, xln_g, ao2 = emit_l0_attn(K, B, t, s, xsrc, xdst)
    emit_l0_attn_attention(K, B, qb, kb, vt)
    K.barrier()
    K.set_scope(f"{base}.wo")
    wq = B.scr.bitcast(BF16)[:, 0:1024]
    wq2 = B.scr.bitcast(BF16)[:, 1024:2048]

    def ev_o(m, tt, ps, tetok):
        tok = K.op("sc", "activation", ao2_list[m][tt], ps, AF.Identity,
                   bias=B.v_abo[:, m:m + 1], waits=[("s_te", tetok)],
                   inc="s_sc")
        return "s_sc", tok
    ao2_list = [[ao2(g, tt) for tt in range(4)] for g in range(NG)]
    rhs = lambda kc, tt: qb[kc][:, tt * 512:(tt + 1) * 512]
    proj_fmajor(K, B, t["a_wo"], 8, rhs, ev_o, [(i, 512) for i in range(4)],
                wq, wq2)
    K.barrier()
    K.set_scope(f"{base}.postln")
    emit_postln_residual(K, B, ao2, xsrc, xdst, s)
    K.barrier()
    K.set_scope(base)



def _np_dt(arr):
    return BF16 if arr.dtype == BF else F32


def build_program(shared, blocks, n_seq=2):
    nc = bass.Bass("TRN2", target_bir_lowering=False, debug=False)
    k = K(nc)
    t = {}
    for name, arr in shared.items():
        t[name] = nc.dram_tensor(name, list(arr.shape), _np_dt(arr),
                                 kind="ExternalInput").ap()
    xin = nc.dram_tensor("xin", [2, 1024, 2048], F32, kind="ExternalInput").ap()
    xout = nc.dram_tensor("xout", [2, 1024, 2048], F32,
                          kind="ExternalOutput").ap()
    chain = [xin]
    for i in range(len(blocks) - 1):
        chain.append(nc.dram_tensor(f"xw{i}", [2, 1024, 2048], F32,
                                    kind="Internal").ap())
    chain.append(xout)
    B = alloc_buffers(nc)
    load_statics(k, B, t)
    k.barrier()
    EM = {"l0a": emit_l0a, "l0f": emit_l0f, "l1a": emit_l1a, "l1f": emit_l1f}
    for s in range(n_seq):
        for bi, blk in enumerate(blocks):
            k.set_scope(f"s{s}.{blk}")
            EM[blk](k, B, t, s, chain[bi], chain[bi + 1])
    k.set_scope(None)
    k.barrier()
    k.build_block()
    return nc


def _run(shared, xs, blocks, cores=8, n_seq=2, **kw):
    nc = build_program(shared, blocks, n_seq=n_seq)
    in_maps = [dict(shared, xin=xs[c]) for c in range(cores)]
    res = run_bass_kernel_spmd(nc, in_maps, core_ids=list(range(cores)), **kw)
    return res


def kernel(**inputs):
    shared, xs = prep_inputs(inputs)
    res = _run(shared, xs, ["l0a", "l0f", "l1a", "l1f"])
    out = np.stack([np.asarray(r["xout"]) for r in res.results])
    out = out.reshape(16, 1024, 2048).transpose(0, 2, 1)
    return np.ascontiguousarray(out.astype(np.float32))


def emit_l0f(K, B, t, s, xsrc, xdst):
    raise NotImplementedError


def emit_l1a(K, B, t, s, xsrc, xdst):
    raise NotImplementedError


def emit_l1f(K, B, t, s, xsrc, xdst):
    raise NotImplementedError


def emit_l1a(K, B, t, s, xsrc, xdst):
    base = K.scope
    ybf = [B.y[g].bitcast(BF16) for g in range(NG)]
    xln = lambda g, tt: ybf[g][:, tt * 512:(tt + 1) * 512]
    xln_g = lambda g: ybf[g][:, 0:2048]
    ao2 = lambda g, tt: ybf[g][:, 2048 + tt * 512:2048 + (tt + 1) * 512]
    qel = [B.poolA[:, g * 2048:(g + 1) * 2048] for g in range(NG)]
    ktm = [B.poolB[:, c * 1024:(c + 1) * 1024] for c in range(16)]
    vtm = [B.poolC[:, c * 1024:(c + 1) * 1024] for c in range(16)]
    aob = [B.poolB[:, g * 2048:(g + 1) * 2048] for g in range(NG)]
    wq = B.scr.bitcast(BF16)[:, 3072:4096]
    wq2 = B.scr.bitcast(BF16)[:, 4096:5120]
    t_tiles = [(i, 512) for i in range(4)]

    K.set_scope(f"{base}.preln")
    emit_preln(K, B, xsrc, s, xln)
    K.barrier()
    K.set_scope(f"{base}.proj")

    def ev_q(m, tt, ps, tetok):
        return elu_evac(K, B, qel[m][:, tt * 512:(tt + 1) * 512], ps, tetok)
    rhs = lambda kc, tt: xln(kc, tt)
    proj_fmajor(K, B, t["b_wq"], 8, rhs, ev_q, t_tiles, wq, wq2)
    def ev_k(tc, sl, ps, gtok):
        return elu_evac(K, B, ktm[tc][:, sl * 512:(sl + 1) * 512], ps, gtok)
    proj_tmajor(K, B, t["b_wk"], xln_g, ev_k)

    def ev_v(tc, sl, ps, gtok):
        return "s_sc", K.op("sc", "activation",
                            vtm[tc][:, sl * 512:(sl + 1) * 512], ps, AF.Copy,
                            waits=[("s_te", gtok)], inc="s_sc")
    proj_tmajor(K, B, t["b_wv"], xln_g, ev_v)
    K.barrier()
    K.set_scope(f"{base}.kv")

    KVsb = B.scr.bitcast(BF16)[:, 2048:3072]
    ksd = B.scr.bitcast(BF16)[:, 0:2048]
    K.op("ve", "memset", KVsb, 0.0)
    kvf = [("s_ve", 0), ("s_ve", 0)]
    ksf = ("s_ve", 0)
    for g in range(NG):
        pkv = B.PSA[:, 512 * (g % 2):512 * (g % 2) + 128]
        K.wait("te", *kvf[g % 2])
        for tc in range(16):
            K.op("te", "matmul", pkv,
                 ktm[tc][:, 128 * g:128 * (g + 1)],
                 vtm[tc][:, 128 * g:128 * (g + 1)],
                 start=(tc == 0), stop=(tc == 15),
                 inc=("s_te" if tc == 15 else None))
        kvtok = K.cnt["s_te"]
        K.wait("te", *ksf)
        for tc in range(16):
            K.op("te", "matmul", B.PSC[:, 0:1],
                 ktm[tc][:, 128 * g:128 * (g + 1)], B.onescol_b,
                 start=(tc == 0), stop=(tc == 15),
                 inc=("s_te" if tc == 15 else None))
        kstok = K.cnt["s_te"]
        for hb in (0, 64):
            tok = K.op("ve", "tensor_copy",
                       KVsb[hb:hb + 64, 128 * g + hb:128 * g + hb + 64],
                       pkv[hb:hb + 64, hb:hb + 64],
                       waits=[("s_te", kvtok)], inc="s_ve")
        kvf[g % 2] = ("s_ve", tok)
        tok2 = K.op("ve", "tensor_tensor", ksd[:, 128 * g:128 * g + 16],
                    B.seg[:, 16 * g:16 * (g + 1)],
                    B.PSC[:, 0:1].broadcast_to((128, 16)), ALU.mult,
                    waits=[("s_te", kstok)], inc="s_ve")
        ksf = ("s_ve", tok2)
    K.barrier()

    K.set_scope(f"{base}.zout")
    psO = [B.PSA[:, 512 * i:512 * (i + 1)] for i in range(2)]
    psZB = [B.PSA[:, 1024 + 512 * i:1024 + 512 * (i + 1)] for i in range(2)]
    of = [("s_sc", 0), ("s_sc", 0)]
    zf = [("s_ve", 0), ("s_ve", 0)]
    for tt in range(4):
        psZ = B.PSB[0:16, 512:1024]
        K.wait("te", "s_ve", K.cnt["s_ve"])
        for g in range(NG):
            K.op("te", "matmul", psZ, ksd[:, 128 * g:128 * g + 16],
                 qel[g][:, tt * 512:(tt + 1) * 512],
                 start=(g == 0), stop=(g == NG - 1),
                 inc=("s_te" if g == NG - 1 else None))
        ztok = K.cnt["s_te"]
        K.op("ve", "tensor_scalar_add", psZ, psZ, 1e-6,
             waits=[("s_te", ztok)])
        zrt = B.scr[0:16, 2304:2816]
        K.op("ve", "reciprocal", zrt, psZ)
        rtok = K.op("ve", "tensor_copy",
                    B.rows16[0:16, 512 * tt:512 * (tt + 1)], zrt, inc="s_ve")
        for g in range(NG):
            pz = psZB[g % 2]
            K.wait("te", *zf[g % 2])
            K.op("te", "matmul", pz, B.exp16[:, 128 * g:128 * (g + 1)],
                 B.rows16[0:16, 512 * tt:512 * (tt + 1)],
                 start=True, stop=True,
                 waits=[("s_ve", rtok)], inc="s_te")
            qtok = K.op("ve", "tensor_tensor", qel[g][:, tt * 512:(tt + 1) * 512],
                        qel[g][:, tt * 512:(tt + 1) * 512], pz, ALU.mult,
                        waits=[("s_te", K.cnt["s_te"])], inc="s_ve")
            zf[g % 2] = ("s_ve", qtok)
            po = psO[g % 2]
            K.wait("te", *of[g % 2])
            K.op("te", "matmul", po, KVsb[:, 128 * g:128 * (g + 1)],
                 qel[g][:, tt * 512:(tt + 1) * 512],
                 start=True, stop=True, waits=[("s_ve", qtok)], inc="s_te")
            otok = K.op("sc", "activation", aob[g][:, tt * 512:(tt + 1) * 512],
                        po, AF.Copy, waits=[("s_te", K.cnt["s_te"])],
                        inc="s_sc")
            of[g % 2] = ("s_sc", otok)
    K.barrier()

    def ev_o(m, tt, ps, tetok):
        return "s_sc", K.op("sc", "activation", ao2(m, tt), ps, AF.Copy,
                            waits=[("s_te", tetok)], inc="s_sc")
    K.set_scope(f"{base}.wo")
    rhs2 = lambda kc, tt: aob[kc][:, tt * 512:(tt + 1) * 512]
    proj_fmajor(K, B, t["b_wo"], 8, rhs2, ev_o, t_tiles, wq, wq2)
    K.barrier()
    K.set_scope(f"{base}.postln")
    emit_postln_residual(K, B, ao2, xsrc, xdst, s)
    K.barrier()
    K.set_scope(base)


def emit_l1f(K, B, t, s, xsrc, xdst):
    ybf = [B.y[g].bitcast(BF16) for g in range(NG)]
    xln = lambda g, tt: ybf[g][:, tt * 512:(tt + 1) * 512]
    xln_g = lambda g: ybf[g][:, 0:2048]
    fout = lambda g, lt: ybf[g][:, 2048 + lt * 512:2048 + (lt + 1) * 512]
    maskb = ybf[0][:, 3072:3136]
    Usb = ybf[1][:, 3072:3328]
    UT = B.y[2].bitcast(BF16)[0:32, 3072:4096]
    scb = B.scr.bitcast(BF16)
    gbuf = scb[:, 5120:5184]
    labuf = scb[:, 4608:4864]
    base = K.scope
    lbbuf = [scb[0:32, 4864:4992], scb[0:32, 4992:5120]]
    K.set_scope(f"{base}.preln")
    emit_preln(K, B, xsrc, s, xln)
    K.dma("gp", gbuf, t["m_gate"])
    K.dma("gp", labuf, t["m_la"])
    K.barrier()
    inter = [(B.poolB[:, c * 1024:(c + 1) * 1024] if c < 16 else
              B.poolC[:, (c - 16) * 1024:(c - 15) * 1024]) for c in range(32)]
    wislab = [B.poolA[:, i * 1024:(i + 1) * 1024] for i in range(2)]
    woslab = [B.poolA[:, 2048 + i * 4096:2048 + (i + 1) * 4096] for i in range(2)]

    for hf in range(2):
        t0 = hf * 1024
        K.set_scope(f"{base}.h{hf}.gate")
        for j in range(8):
            tc = hf * 8 + j
            psZ = B.PSC[:, 0:8]
            psT = B.PSB[:, 0:32]
            K.wait("te", "s_ve", K.cnt["s_ve"])
            for kc in range(NG):
                K.op("te", "matmul", psZ,
                     xln_g(kc)[:, tc * 128:(tc + 1) * 128],
                     gbuf[:, kc * 8:(kc + 1) * 8],
                     start=(kc == 0), stop=(kc == NG - 1),
                     inc=("s_te" if kc == NG - 1 else None))
            ztok = K.cnt["s_te"]
            for kc in range(NG):
                K.op("te", "matmul", psT,
                     xln_g(kc)[:, tc * 128:(tc + 1) * 128],
                     labuf[:, kc * 32:(kc + 1) * 32],
                     start=(kc == 0), stop=(kc == NG - 1),
                     inc=("s_te" if kc == NG - 1 else None))
            ttok = K.cnt["s_te"]
            zf = B.scr[:, 0:8]
            m8 = B.scr[:, 8:9]
            K.op("ve", "tensor_add", zf, psZ, B.v_gbc, waits=[("s_te", ztok)])
            K.op("ve", "reduce_max", m8, zf, axis=AX.X)
            mk = maskb[:, j * 8:(j + 1) * 8]
            K.op("ve", "tensor_scalar", mk, zf, m8[:, 0:1], None, ALU.is_equal)
            uu = Usb[:, j * 32:(j + 1) * 32]
            vtok = K.op("ve", "tensor_tensor",
                        uu.rearrange("p (e r) -> p e r", r=4),
                        psT.rearrange("p (e r) -> p e r", r=4),
                        mk.unsqueeze(2).broadcast_to((128, 8, 4)), ALU.mult,
                        waits=[("s_te", ttok)], inc="s_ve")
            trtok = K.op("te", "transpose", B.PST[0:32, j * 128:(j + 1) * 128],
                         uu, B.ident, waits=[("s_ve", vtok)], inc="s_te")
        K.op("ve", "tensor_copy", UT, B.PST[0:32, 0:1024],
             waits=[("s_te", K.cnt["s_te"])], inc="s_ve")
        K.barrier(("te", "ve", "sc", "gp"))
        K.set_scope(f"{base}.h{hf}.p1")
        wtok = [("d_w0", 0), ("d_w1", 0)]
        wfree = [K.cnt["s_te"]] * 2
        banks = [B.PSA[:, 512 * i:512 * (i + 1)] for i in range(4)]
        bfree = [("s_sc", K.cnt["s_sc"])] * 4
        bi = 0
        for fc in range(32):
            wb = fc % 2
            K.wait("gp", "s_te", wfree[wb])
            K.dma("gp", wislab[wb], t["m_wi"][:, fc * 1024:(fc + 1) * 1024],
                  sem=f"d_w{wb}")
            wtok[wb] = K.dma("gp", lbbuf[wb],
                             t["m_lb"][0:32, fc * 128:(fc + 1) * 128],
                             sem=f"d_w{wb}")
            for tt2 in range(2):
                ps = banks[bi % 4]
                K.wait("te", *wtok[wb])
                K.wait("te", *bfree[bi % 4])
                for kc in range(NG):
                    K.op("te", "matmul", ps, wislab[wb][:, kc * 128:(kc + 1) * 128],
                         xln_g(kc)[:, t0 + tt2 * 512:t0 + (tt2 + 1) * 512],
                         start=(kc == 0), stop=False)
                K.op("te", "matmul", ps, lbbuf[wb],
                     UT[0:32, tt2 * 512:(tt2 + 1) * 512],
                     start=False, stop=True, inc="s_te")
                tetok = K.cnt["s_te"]
                if tt2 == 1:
                    wfree[wb] = tetok
                stok = K.op("sc", "activation",
                            inter[fc][:, tt2 * 512:(tt2 + 1) * 512], ps,
                            AF.Relu, bias=B.v_wib[:, fc:fc + 1],
                            waits=[("s_te", tetok)], inc="s_sc")
                bfree[bi % 4] = ("s_sc", stok)
                bi += 1
        K.barrier(("te", "ve", "sc", "gp"))
        K.set_scope(f"{base}.h{hf}.p2")
        wtok = [("d_w0", 0), ("d_w1", 0)]
        wfree = [K.cnt["s_te"]] * 2
        bfree = [("s_sc", K.cnt["s_sc"])] * 4
        bi = 0
        for m in range(8):
            wb = m % 2
            K.wait("gp", "s_te", wfree[wb])
            wtok[wb] = K.dma("gp", woslab[wb],
                             t["m_wo"][:, m * 4096:(m + 1) * 4096],
                             sem=f"d_w{wb}")
            for tt2 in range(2):
                ps = banks[bi % 4]
                K.wait("te", *wtok[wb])
                K.wait("te", *bfree[bi % 4])
                for fc in range(32):
                    K.op("te", "matmul", ps, woslab[wb][:, fc * 128:(fc + 1) * 128],
                         inter[fc][:, tt2 * 512:(tt2 + 1) * 512],
                         start=(fc == 0), stop=(fc == 31),
                         inc=("s_te" if fc == 31 else None))
                tetok = K.cnt["s_te"]
                if tt2 == 1:
                    wfree[wb] = tetok
                lt = hf * 2 + tt2
                stok = K.op("sc", "activation", fout(m, lt % 2), ps,
                            AF.Identity, bias=B.v_wob[:, m:m + 1],
                            waits=[("s_te", tetok)], inc="s_sc")
                bfree[bi % 4] = ("s_sc", stok)
                bi += 1
        K.barrier(("te", "ve", "sc", "gp", "sy"))
        K.set_scope(f"{base}.h{hf}.postln")
        emit_postln_residual(K, B, lambda g, tt: fout(g, tt - 2 * hf),
                             xsrc, xdst, s, tts=(2 * hf, 2 * hf + 1))
        K.barrier()
    K.set_scope(base)


def emit_l0f(K, B, t, s, xsrc, xdst):
    base = K.scope
    y = B.y
    yv = lambda g, tt: y[g][:, 1 + tt * 512:1 + (tt + 1) * 512]
    pfb = B.poolF.bitcast(BF16)
    qb = [B.poolA[:, g * 2048:(g + 1) * 2048] for g in range(NG)]
    kb = [B.poolB[:, g * 2048:(g + 1) * 2048] for g in range(NG)]
    vt = [B.poolC[:, c * 1024:(c + 1) * 1024] for c in range(16)]
    scb = B.scr.bitcast(BF16)
    K.set_scope(f"{base}.preln")
    for g in range(NG):
        K.op("ve", "memset", y[g][:, 0:1], 0.0)
    emit_preln(K, B, xsrc, s, None, out_f32=yv)
    K.barrier()

    for li in range(6):
        K.dma("sy", B.v_b1a, t["f_b1a"][:, 22 * li:22 * (li + 1)],
              waits=[("s_sc", K.cnt["s_sc"]), ("s_ve", K.cnt["s_ve"])])
        K.dma("sy", B.v_b1g, t["f_b1g"][:, 22 * li:22 * (li + 1)])
        for hf in range(2):
            t0 = hf * 1024
            K.set_scope(f"{base}.L{li}h{hf}.qkv")
            K.dma("sy", B.tabs[:, 0:1024], t["tab_ct"][:, t0:t0 + 1024],
                  waits=[("s_ve", K.cnt["s_ve"])])
            K.dma("sy", B.tabs[:, 1024:2048], t["tab_st"][:, t0:t0 + 1024])
            K.dma("sy", B.tabs[:, 2048:3072], t["tab_cti"][:, t0:t0 + 1024])
            K.dma("sy", B.tabs[:, 3072:4096], t["tab_sti"][:, t0:t0 + 1024])
            ttabs = ("d_sy", K.cnt["d_sy"])
            ysh = [pfb[:, g * 1024:(g + 1) * 1024] for g in range(NG)]
            shtok = 0
            for g in range(NG):
                off = (1 + t0) if g < 4 else t0
                shtok = K.op("gp", "tensor_copy", ysh[g],
                             y[g][:, off:off + 1024], inc="s_gp")
            sq2 = [scb[:, 2048:2560], scb[:, 2560:3072]]
            psN = [B.PSB[0:16, 0:512], B.PSB[0:16, 512:1024]]
            nfree = [("s_ve", K.cnt["s_ve"]), ("s_ve", K.cnt["s_ve"])]
            sqi = [0]

            def ev_qk(buf, scol, ptab):
                def f(m, tt2, ps, tetok):
                    j = sqi[0] % 2
                    sqi[0] += 1
                    stok = K.op("sc", "activation", sq2[j], ps, AF.Square,
                                waits=[("s_te", tetok)], inc="s_sc")
                    if m == 0:
                        K.wait("te", *nfree[tt2])
                    K.op("te", "matmul", psN[tt2], B.seg[:, 16 * m:16 * (m + 1)],
                         sq2[j], start=(m == 0), stop=(m == NG - 1),
                         waits=[("s_sc", stok)], inc="s_te")
                    etok = K.op("sc", "activation",
                                buf[m][:, t0 + tt2 * 512:t0 + (tt2 + 1) * 512],
                                ps, AF.Copy, scale=scol,
                                waits=[], inc="s_sc")
                    return "s_sc", etok
                return f
            rhs_sh = lambda kc, tt2: ysh[kc][:, tt2 * 512:(tt2 + 1) * 512]
            wqb = [scb[:, 3584:4608], scb[:, 4608:5632]]
            proj_fmajor(K, B, t[f"f_wq{li}"], 8, rhs_sh,
                        ev_qk(qb, B.v_qs[:, li:li + 1], None),
                        [(0, 512), (1, 512)], wqb[0], wqb[1])
            zr = B.scr[0:16, 0:512]
            for tt2 in range(2):
                stok = K.op("sc", "activation", psN[tt2], psN[tt2], AF.Sqrt,
                            waits=[("s_te", K.cnt["s_te"])], inc="s_sc")
                K.op("ve", "tensor_scalar_max", psN[tt2], psN[tt2], 1e-12,
                     waits=[("s_sc", stok)])
                K.op("ve", "reciprocal", zr, psN[tt2])
                ntok = K.op("ve", "tensor_scalar_mul",
                            B.rows16[0:16, t0 + tt2 * 512:t0 + (tt2 + 1) * 512],
                            zr, 8.0, inc="s_ve")
                nfree[tt2] = ("s_ve", ntok)
            K.set_scope(f"{base}.L{li}h{hf}.kproj")
            sqi[0] = 0
            proj_fmajor(K, B, t[f"f_wk{li}"], 8, rhs_sh,
                        ev_qk(kb, B.v_ks[:, li:li + 1], None),
                        [(0, 512), (1, 512)], wqb[0], wqb[1])
            for tt2 in range(2):
                stok = K.op("sc", "activation", psN[tt2], psN[tt2], AF.Sqrt,
                            waits=[("s_te", K.cnt["s_te"])], inc="s_sc")
                K.op("ve", "tensor_scalar_max", psN[tt2], psN[tt2], 1e-12,
                     waits=[("s_sc", stok)])
                K.op("ve", "reciprocal", zr, psN[tt2])
                ntok = K.op("ve", "tensor_copy",
                            scb[0:16, 2048 + tt2 * 512:2048 + (tt2 + 1) * 512],
                            zr, inc="s_ve")
                nfree[tt2] = ("s_ve", ntok)
            ve_pre_rot = K.cnt["s_ve"]
            K.set_scope(f"{base}.L{li}h{hf}.vrot")
            rot = RotarySteps(K, B, ttabs)
            rot.add(qb, t0, 0, 1024, B.rows16[0:16, t0:t0 + 1024])
            rot.add(kb, t0, 2048, 3072, scb[0:16, 2048:3072])
            def ev_v(tcl, sl, ps, gtok2):
                tok = K.op("sc", "activation",
                           vt[hf * 8 + tcl][:, sl * 512:(sl + 1) * 512],
                           ps, AF.Copy, waits=[("s_te", gtok2)],
                           inc="s_sc")
                return "s_sc", tok
            proj_tmajor(K, B, t[f"f_wv{li}"], lambda kc: ysh[kc], ev_v,
                        tcs_all=list(range(8)), ve_guard=ve_pre_rot,
                        co_fn=lambda: rot.step(2))
            rot.step(len(rot.todo))
            K.barrier(("te", "ve", "sc", "gp", "sy"))
        K.set_scope(f"{base}.L{li}.attn")
        _l0f_attention(K, B, qb, kb, vt)
        K.barrier()
        K.set_scope(f"{base}.L{li}.wo")
        def ev_o(m, tt, ps, tetok):
            tok = K.op("ve", "tensor_tensor", yv(m, tt), ps, yv(m, tt),
                       ALU.add, waits=[("s_te", tetok)], inc="s_ve")
            return "s_ve", tok
        proj_fmajor(K, B, t[f"f_wo{li}"], 8,
                    lambda kc, tt: qb[kc][:, tt * 512:(tt + 1) * 512],
                    ev_o, [(i, 512) for i in range(4)],
                    scb[:, 3584:4608], scb[:, 4608:5632])
        K.barrier()
        for hf in range(2):
            t0 = hf * 1024
            K.set_scope(f"{base}.L{li}h{hf}.ffnln")
            xn = [pfb[:, g * 1024:(g + 1) * 1024] for g in range(NG)]
            sqb = B.scr.bitcast(BF16)[:, 2048:2560]
            psA = B.PSA[:, 0:512]
            psB = B.PSA[:, 512:1024]
            tmpf = B.scr[:, 1280:1792]
            for tt2 in range(2):
                srcs = [y[g][:, (1 + t0 + tt2 * 512 - (0 if g < 4 else 1)):
                             (1 + t0 + (tt2 + 1) * 512 - (0 if g < 4 else 1))]
                        for g in range(NG)]
                tetok = emit_ln_stats(K, B, srcs, 512, sqb, psA, psB,
                                      waits=[("s_ve", K.cnt["s_ve"]),
                                             ("s_sc", K.cnt["s_sc"])])
                a, c, vtok = emit_ln_factors(K, B, psA, psB, 512, tetok)
                tmpd = [tmpf, B.scr[:, 1792:2304]]
                sct = [K.cnt["s_sc"]] * 2
                for g in range(NG):
                    tf = tmpd[g % 2]
                    K.op("ve", "tensor_mul", tf, srcs[g], a,
                         waits=[("s_sc", sct[g % 2])])
                    vtok = K.op("ve", "tensor_sub", tf, tf, c, inc="s_ve")
                    sct[g % 2] = K.op("sc", "activation",
                         xn[g][:, tt2 * 512:(tt2 + 1) * 512], tf,
                         AF.Identity, scale=B.v_g[:, 8 * li + g:8 * li + g + 1],
                         bias=B.v_b[:, 8 * li + g:8 * li + g + 1],
                         waits=[("s_ve", vtok)], inc="s_sc")
            K.barrier(("te", "ve", "sc", "gp"))
            K.set_scope(f"{base}.L{li}h{hf}.w1")
            w2a = B.poolA[:, 4096:16384]
            w2c = B.poolC[:, 6144:16384]
            w2tok1 = K.dma("gp", w2a, t[f"f_w2{li}"][:, 0:12288])
            w2tok2 = K.dma("gp", w2c, t[f"f_w2{li}"][:, 12288:22528])

            def w2tile(m, kc):
                flat = m * 2816 + kc * 128
                return (w2a[:, flat:flat + 128] if flat < 12288 else
                        w2c[:, flat - 12288:flat - 12288 + 128])
            inter = [(B.poolB[:, cc * 1024:(cc + 1) * 1024] if cc < 16 else
                      B.poolC[:, (cc - 16) * 1024:(cc - 15) * 1024])
                     for cc in range(22)]
            w1a = [B.poolA[:, 0:1024], B.poolA[:, 1024:2048]]
            w1g = [B.poolA[:, 2048:3072], B.poolA[:, 3072:4096]]
            gel2 = [scb[:, 2560:3072], scb[:, 3072:3584]]
            gelf = [K.cnt["s_ve"]] * 2
            gj = [0]
            wtok = [("d_w0", 0), ("d_w1", 0)]
            wfree = [K.cnt["s_te"]] * 2
            banks = [B.PSA[:, 512 * i:512 * (i + 1)] for i in range(4)]
            bfree = [("s_ve", K.cnt["s_ve"])] * 4
            K.wait("te", "s_sc", K.cnt["s_sc"])
            bi = 0
            for fc in range(22):
                wb = fc % 2
                K.wait("gp", "s_te", wfree[wb])
                K.dma("gp", w1a[wb], t[f"f_w1a{li}"][:, fc * 1024:(fc + 1) * 1024],
                      sem=f"d_w{wb}")
                wtok[wb] = K.dma("gp", w1g[wb],
                                 t[f"f_w1g{li}"][:, fc * 1024:(fc + 1) * 1024],
                                 sem=f"d_w{wb}")
                for tt2 in range(2):
                    psa_ = banks[bi % 4]
                    psg_ = banks[(bi + 1) % 4]
                    K.wait("te", *wtok[wb])
                    K.wait("te", *bfree[bi % 4])
                    K.wait("te", *bfree[(bi + 1) % 4])
                    for kc in range(NG):
                        K.op("te", "matmul", psa_,
                             w1a[wb][:, kc * 128:(kc + 1) * 128],
                             xn[kc][:, tt2 * 512:(tt2 + 1) * 512],
                             start=(kc == 0), stop=(kc == NG - 1),
                             inc=("s_te" if kc == NG - 1 else None))
                    atok = K.cnt["s_te"]
                    for kc in range(NG):
                        K.op("te", "matmul", psg_,
                             w1g[wb][:, kc * 128:(kc + 1) * 128],
                             xn[kc][:, tt2 * 512:(tt2 + 1) * 512],
                             start=(kc == 0), stop=(kc == NG - 1),
                             inc=("s_te" if kc == NG - 1 else None))
                    gtok = K.cnt["s_te"]
                    if tt2 == 1:
                        wfree[wb] = gtok
                    j = gj[0] % 2
                    gj[0] += 1
                    gel = gel2[j]
                    stok = K.op("sc", "activation", gel, psg_, AF.Gelu,
                                bias=B.v_b1g[:, fc:fc + 1],
                                waits=[("s_te", gtok), ("s_ve", gelf[j])],
                                inc="s_sc")
                    vtok = K.op("ve", "scalar_tensor_tensor",
                                inter[fc][:, tt2 * 512:(tt2 + 1) * 512],
                                psa_, B.v_b1a[:, fc:fc + 1], gel,
                                ALU.add, ALU.mult,
                                waits=[("s_te", atok), ("s_sc", stok)],
                                inc="s_ve")
                    gelf[j] = vtok
                    bfree[bi % 4] = ("s_ve", vtok)
                    bfree[(bi + 1) % 4] = ("s_ve", vtok)
                    bi += 2
            K.barrier(("te", "ve", "sc", "gp"))
            K.set_scope(f"{base}.L{li}h{hf}.w2")
            bfree = [("s_ve", K.cnt["s_ve"])] * 4
            K.wait("te", *w2tok2)
            bi = 0
            for m in range(8):
                for tt2 in range(2):
                    ps = banks[bi % 4]
                    K.wait("te", *bfree[bi % 4])
                    for kc in range(22):
                        K.op("te", "matmul", ps, w2tile(m, kc),
                             inter[kc][:, tt2 * 512:(tt2 + 1) * 512],
                             start=(kc == 0), stop=(kc == 21),
                             inc=("s_te" if kc == 21 else None))
                    tetok = K.cnt["s_te"]
                    ysl = y[m][:, 1 + t0 + tt2 * 512:1 + t0 + (tt2 + 1) * 512]
                    vtok = K.op("ve", "scalar_tensor_tensor", ysl, ps,
                                B.v_b2[:, 8 * li + m:8 * li + m + 1], ysl,
                                ALU.add, ALU.add,
                                waits=[("s_te", tetok)], inc="s_ve")
                    bfree[bi % 4] = ("s_ve", vtok)
                    bi += 1
            K.barrier(("te", "ve", "sc", "gp"))
    K.barrier()
    K.set_scope(f"{base}.postln")
    emit_postln_residual(K, B, yv, xsrc, xdst, s)
    K.barrier()
    K.set_scope(base)


class RotarySteps:

    def __init__(self, K, B, ttok):
        self.K, self.B, self.ttok = K, B, ttok
        scb = B.scr.bitcast(BF16)
        self.tmp1 = scb[:, 0:512]
        self.tmp2 = scb[:, 512:1024]
        self.psR = B.PSC[:, 0:512]
        self.psBc = B.PST.bitcast(F32)[:, 0:512]
        self.sc_guard = K.cnt["s_sc"]
        self.rfree = ("s_ve", K.cnt["s_ve"])
        self.bfree = ("s_ve", K.cnt["s_ve"])
        self.todo = []

    def add(self, buf, t0, bc, bs, rows):
        for g in range(NG):
            for tt2 in range(2):
                self.todo.append((buf, t0, bc, bs, rows, g, tt2))

    def step(self, n=1):
        K, B = self.K, self.B
        for _ in range(n):
            if not self.todo:
                return
            buf, t0, bc, bs, rows, g, tt2 = self.todo.pop(0)
            sl = buf[g][:, t0 + tt2 * 512:t0 + (tt2 + 1) * 512]
            K.wait("te", "s_sc", self.sc_guard)
            K.wait("te", *self.rfree)
            rtok = K.op("te", "matmul", self.psR, B.rot, sl,
                        start=True, stop=True, waits=[self.ttok], inc="s_te")
            K.wait("te", *self.bfree)
            btok = K.op("te", "matmul", self.psBc,
                        B.exp16[:, 128 * g:128 * (g + 1)],
                        rows[:, tt2 * 512:(tt2 + 1) * 512],
                        start=True, stop=True, inc="s_te")
            K.op("ve", "tensor_tensor", self.tmp1, sl,
                 B.tabs[:, bc + tt2 * 512:bc + (tt2 + 1) * 512], ALU.mult,
                 waits=[("s_te", rtok)])
            K.op("ve", "tensor_tensor", self.tmp2, self.psR,
                 B.tabs[:, bs + tt2 * 512:bs + (tt2 + 1) * 512], ALU.mult,
                 waits=[("s_te", btok)])
            self.rfree = ("s_ve", K.cnt["s_ve"])
            K.op("ve", "tensor_add", self.tmp1, self.tmp1, self.tmp2)
            vtok = K.op("ve", "tensor_tensor", sl, self.tmp1, self.psBc,
                        ALU.mult, inc="s_ve")
            self.bfree = ("s_ve", vtok)


def _l0f_attention(K, B, qb, kb, vt):
    pfb = B.poolF.bitcast(BF16)
    C = B.col
    PSTs = [B.PST, B.PSB.bitcast(BF16)[:, 0:1024]]
    PSCs = [B.PSC[:, 0:256], B.PSB[:, 512:768]]
    units = [(g, h, w) for g in range(NG) for h in range(2) for w in range(8)]
    n = len(units)
    te0, ve0, sc0 = K.cnt["s_te"], K.cnt["s_ve"], K.cnt["s_sc"]
    stok = [[te0, te0] for _ in range(n)]
    mx_tok = [[ve0, ve0] for _ in range(n)]
    exp_tok = [sc0] * n
    norm_tok = [ve0] * n
    tr_tok = [te0] * n
    ev_tok = [sc0] * n
    pv_tok = [te0] * n
    out_tok = [ve0] * n

    def geom(u):
        g, h, w = units[u]
        if w > 0:
            subs = [(256 * (w - 1), 384, B.cmask),
                    (256 * (w - 1) + 128, 384, B.cmask)]
            kcs = [(0, 1, 2), (1, 2, 3)]
        else:
            subs = [(0, 128, B.w0m0), (0, 256, B.w0m1)]
            kcs = [(0,), (0, 1)]
        return g, h, 64 * h, w, subs, kcs

    def front_te(u):
        g, h, hb, w, subs, kcs = geom(u)
        if u >= 2:
            K.wait("te", "s_sc", exp_tok[u - 2])
        for sub, (koff, kwid, mask) in enumerate(subs):
            ps = B.PSA[:, (u % 2) * 1024 + sub * 512:
                       (u % 2) * 1024 + sub * 512 + kwid]
            tok = K.op("te", "matmul", ps,
                       qb[g][hb:hb + 64, w * 256 + sub * 128:
                             w * 256 + (sub + 1) * 128],
                       kb[g][hb:hb + 64, koff:koff + kwid],
                       start=True, stop=(sub == 0),
                       inc=("s_te" if sub == 0 else None))
            if sub == 1:
                tok = K.op("te", "matmul", ps, B.ident, mask,
                           start=False, stop=True, inc="s_te")
            stok[u][sub] = tok

    def mid(u):
        g, h, hb, w, subs, kcs = geom(u)
        cb = (u % 4) * 4
        slot = 1024 * (u % 4)
        K.op("ve", "reciprocal", C[:, cb + 2:cb + 4], C[:, cb + 2:cb + 4],
             waits=[("s_sc", exp_tok[u])])
        nm = 0
        for sub, (koff, kwid, mask) in enumerate(subs):
            Pp = pfb[:, slot + sub * 512:slot + sub * 512 + kwid]
            nm = K.op("ve", "tensor_scalar_mul", Pp, Pp,
                      C[:, cb + 2 + sub:cb + 3 + sub], inc="s_ve")
        pst = PSTs[u % 2]
        K.wait("te", "s_sc", ev_tok[u - 2] if u >= 2 else sc0)
        first = True
        nkc_all = sorted(set(kcs[0]) | set(kcs[1]))
        for kc in nkc_all:
            for sub in range(2):
                if kc not in kcs[sub]:
                    continue
                koff = subs[sub][0]
                rel = kc - (koff - subs[0][0]) // 128 if w > 0 else kc
                K.op("te", "transpose",
                     pst[:, 256 * kc + 128 * sub:256 * kc + 128 * sub + 128],
                     pfb[:, slot + sub * 512 + 128 * rel:
                         slot + sub * 512 + 128 * (rel + 1)],
                     B.ident,
                     waits=[("s_ve", nm)] if first else [],
                     inc="s_te")
                first = False
        tr_tok[u] = K.cnt["s_te"]
        PT = pfb[:, 4096 + 1024 * (u % 2):4096 + 1024 * (u % 2) + 1024]
        wd = 256 * (max(nkc_all) + 1)
        w_ = [("s_te", tr_tok[u])]
        if u >= 2:
            w_.append(("s_te", pv_tok[u - 2]))
        evv = K.op("sc", "activation", PT[:, 0:wd], pst[:, 0:wd], AF.Copy,
                   waits=w_, inc="s_sc")
        ev_tok[u] = evv
        psc = PSCs[u % 2]
        c0 = 2 * (w - 1) if w > 0 else 0
        if u >= 2:
            K.wait("te", "s_ve", out_tok[u - 2])
        for sub in range(2):
            ks = kcs[sub]
            for i, kc in enumerate(ks):
                pv_tok[u] = K.op(
                    "te", "matmul", psc[:, sub * 128:(sub + 1) * 128],
                    vt[c0 + kc][:, g * 128:(g + 1) * 128],
                    PT[:, 256 * kc + 128 * sub:256 * kc + 128 * sub + 128],
                    start=(i == 0), stop=(i == len(ks) - 1),
                    waits=[("s_sc", evv)],
                    inc=("s_te" if i == len(ks) - 1 else None))

    def tail(u):
        g, h, hb, w, subs, kcs = geom(u)
        out_tok[u] = K.op("ve", "tensor_copy",
                          qb[g][hb:hb + 64, w * 256:(w + 1) * 256],
                          PSCs[u % 2][hb:hb + 64, 0:256],
                          waits=[("s_te", pv_tok[u])], inc="s_ve")

    def front_rest(u):
        g, h, hb, w, subs, kcs = geom(u)
        cb = (u % 4) * 4
        slot = 1024 * (u % 4)
        for sub, (koff, kwid, mask) in enumerate(subs):
            ps = B.PSA[:, (u % 2) * 1024 + sub * 512:
                       (u % 2) * 1024 + sub * 512 + kwid]
            if sub == 0:
                K.op("ve", "tensor_tensor", ps, ps, mask, ALU.add,
                     waits=[("s_te", stok[u][sub])])
                mx_tok[u][sub] = K.op("ve", "reduce_max",
                                      C[:, cb + sub:cb + sub + 1],
                                      ps, axis=AX.X, negate=True, inc="s_ve")
            else:
                mx_tok[u][sub] = K.op("ve", "reduce_max",
                                      C[:, cb + sub:cb + sub + 1],
                                      ps, axis=AX.X, negate=True,
                                      waits=[("s_te", stok[u][sub])],
                                      inc="s_ve")
            Pp = pfb[:, slot + sub * 512:slot + sub * 512 + kwid]
            w_ = [("s_ve", mx_tok[u][sub])]
            if sub == 0 and u >= 4:
                w_.append(("s_te", tr_tok[u - 4]))
            exp_tok[u] = K.op("sc", "activation", Pp, ps, AF.Exp,
                              bias=C[:, cb + sub:cb + sub + 1],
                              accum_out=C[:, cb + 2 + sub:cb + 3 + sub],
                              waits=w_, inc="s_sc")

    for u in range(n):
        front_te(u)
        if u >= 1:
            mid(u - 1)
        if u >= 2:
            tail(u - 2)
        front_rest(u)
    mid(n - 1)
    tail(n - 2)
    tail(n - 1)


# version 10 (final)
# speedup vs baseline: 1.0182x; 1.0182x over previous
import numpy as np
import ml_dtypes

import concourse.bass as bass
import concourse.mybir as mybir
from concourse.bass_utils import run_bass_kernel_spmd

F32 = mybir.dt.float32
BF16 = mybir.dt.bfloat16
AF = mybir.ActivationFunctionType
ALU = mybir.AluOpType
AX = mybir.AxisListType
BF = ml_dtypes.bfloat16

D = 1024
T = 2048
NG = 8
NEG = np.float32(-3.0e38)


def _pm(w, mchunk):
    K, M = w.shape
    assert K % 128 == 0 and M % mchunk == 0, (K, M, mchunk)
    nk, nm = K // 128, M // mchunk
    r = w.reshape(nk, 128, nm, mchunk).transpose(1, 2, 0, 3)
    return np.ascontiguousarray(r.reshape(128, nm * nk * mchunk))


def _padc(w, cols):
    out = np.zeros((w.shape[0], cols), np.float32)
    out[:, : w.shape[1]] = w
    return out


def _padr(w, rows):
    out = np.zeros((rows, w.shape[1]), np.float32)
    out[: w.shape[0], :] = w
    return out


def _cols(v):
    assert v.shape[0] % 128 == 0
    return np.ascontiguousarray(v.reshape(-1, 128).T.astype(np.float32))


def _rotary_tables():
    DH, W, n = 64, 256, T
    half = np.arange(0, DH, 2, dtype=np.float32)
    inv_freq = (1.0 / (10000.0 ** (half / DH))).astype(np.float32)
    t = np.arange(n, dtype=np.float32)
    fr = t[:, None] * inv_freq[None, :]
    freqs = np.concatenate([fr, fr], -1)
    base = ((half + 0.4 * DH) / (1.4 * DH)).astype(np.float32)
    power = ((t - n // 2) / (W // 2)).astype(np.float32)
    sc = base[None, :] ** power[:, None]
    scale = np.concatenate([sc, sc], -1).astype(np.float32)
    cos, sin = np.cos(freqs), np.sin(freqs)
    ct = (cos * scale).T
    st = (sin * scale).T
    cti = (cos / scale).T
    sti = (sin / scale).T
    f = lambda a: np.ascontiguousarray(np.tile(a, (2, 1)).astype(BF))
    return f(ct), f(st), f(cti), f(sti)


def _host_constants():
    c = {}
    c["ident"] = np.eye(128, dtype=BF)
    R = np.zeros((128, 128), np.float32)
    for b in (0, 64):
        for dd in range(32):
            R[b + dd + 32, b + dd] = -1.0
            R[b + dd, b + dd + 32] = 1.0
    c["rot"] = R.astype(BF)
    ct, st, cti, sti = _rotary_tables()
    c["tab_ct"], c["tab_st"], c["tab_cti"], c["tab_sti"] = ct, st, cti, sti
    seg = np.zeros((128, 128), np.float32)
    for m in range(8):
        seg[0:64, m * 16 + 2 * m] = 1.0
        seg[64:128, m * 16 + 2 * m + 1] = 1.0
    c["seg"] = seg.astype(BF)
    e = np.zeros((16, 128 * 8), np.float32)
    for g in range(8):
        e[2 * g, g * 128 + 0:g * 128 + 64] = 1.0
        e[2 * g + 1, g * 128 + 64:g * 128 + 128] = 1.0
    c["exp16"] = e.astype(BF)
    r = np.arange(128)[:, None]
    j1 = np.arange(128)[None, :]
    tri_ge = np.where(j1 >= r, 0.0, NEG)
    tri_le = np.where(j1 <= r, 0.0, NEG)
    c["cmask"] = np.concatenate(
        [tri_ge, np.zeros((128, 128), np.float32), tri_le], 1).astype(BF)
    c["w0m0"] = tri_le.astype(BF)
    c["w0m1"] = np.concatenate(
        [np.zeros((128, 128), np.float32), tri_le], 1).astype(BF)
    mwp = np.full((128, 256), NEG, np.float32)
    mwp[0:64, 0:192] = 0.0
    mwp[64:128, 64:256] = 0.0
    c["mwpair"] = mwp.astype(BF)
    c["ones_f"] = np.ones((128, 128), np.float32)
    c["ones_b"] = np.ones((128, 128), dtype=BF)
    c["onescol_b"] = np.ones((128, 1), dtype=BF)
    return c


def prep_inputs(I):
    d = dict(_host_constants())
    d["a_wq"] = _pm(I["L0_attn_Wq"], 128)
    d["a_wk"] = _pm(I["L0_attn_Wk"], 128)
    d["a_wv"] = _pm(I["L0_attn_Wv"], 512)
    d["a_wo"] = _pm(I["L0_attn_Wo"], 128)
    d["a_bq"] = _cols(I["L0_attn_bq"])
    d["a_bk"] = _cols(I["L0_attn_bk"])
    d["a_bv"] = _cols(I["L0_attn_bv"])
    d["a_bo"] = _cols(I["L0_attn_bo"])
    qkv = I["L0_ffn_Wqkv"]
    for i in range(6):
        d[f"f_wq{i}"] = _pm(qkv[i][:, 0:1024], 128)
        d[f"f_wk{i}"] = _pm(qkv[i][:, 1024:2048], 128)
        d[f"f_wv{i}"] = _pm(qkv[i][:, 2048:3072], 512)
        d[f"f_wo{i}"] = _pm(I["L0_ffn_Wo"][i], 128)
        d[f"f_w1a{i}"] = _pm(_padc(I["L0_ffn_W1"][i][:, 0:2730], 2816), 128)
        d[f"f_w1g{i}"] = _pm(_padc(I["L0_ffn_W1"][i][:, 2730:5460], 2816), 128)
        d[f"f_w2{i}"] = _pm(_padr(I["L0_ffn_W2"][i], 2816), 128)
    d["f_qs"] = np.ascontiguousarray(
        np.tile(I["L0_ffn_q_scale"], (1, 2)).T.astype(np.float32))
    d["f_ks"] = np.ascontiguousarray(
        np.tile(I["L0_ffn_k_scale"], (1, 2)).T.astype(np.float32))
    d["f_g"] = np.concatenate([_cols(I["L0_ffn_ln_g"][i]) for i in range(6)], 1)
    d["f_b"] = np.concatenate([_cols(I["L0_ffn_ln_b"][i]) for i in range(6)], 1)
    d["f_b1a"] = np.concatenate(
        [_cols(_padc(I["L0_ffn_b1"][i][None, 0:2730], 2816)[0]) for i in range(6)], 1)
    d["f_b1g"] = np.concatenate(
        [_cols(_padc(I["L0_ffn_b1"][i][None, 2730:5460], 2816)[0]) for i in range(6)], 1)
    d["f_b2"] = np.concatenate([_cols(I["L0_ffn_b2"][i]) for i in range(6)], 1)
    d["b_wq"] = _pm(np.ascontiguousarray(I["L1_attn_Wq"].T), 128)
    d["b_wk"] = _pm(np.ascontiguousarray(I["L1_attn_Wk"].T), 512)
    d["b_wv"] = _pm(np.ascontiguousarray(I["L1_attn_Wv"].T), 512)
    d["b_wo"] = _pm(np.ascontiguousarray(I["L1_attn_Wo"].T), 128)
    d["m_gate"] = _pm(np.ascontiguousarray(I["L1_ffn_gate_W"].T), 8)
    d["m_gbc"] = np.ascontiguousarray(
        np.broadcast_to(I["L1_ffn_gate_b"], (128, 8)).astype(np.float32))
    d["m_wi"] = _pm(np.ascontiguousarray(I["L1_ffn_wi_W"].T), 128)
    d["m_wo"] = _pm(np.ascontiguousarray(I["L1_ffn_wo_W"].T), 128)
    d["m_wib"] = _cols(I["L1_ffn_wi_b"])
    d["m_wob"] = _cols(I["L1_ffn_wo_b"])
    d["m_la"] = _pm(np.ascontiguousarray(I["L1_ffn_lora_A"].reshape(32, 1024).T), 32)
    d["m_lb"] = np.ascontiguousarray(
        I["L1_ffn_lora_B"].transpose(0, 2, 1).reshape(32, 4096).astype(np.float32))
    x = I["L0_attn_x"]
    xt = np.ascontiguousarray(x.transpose(0, 2, 1))
    xs = [np.ascontiguousarray(xt[2 * c:2 * c + 2]) for c in range(8)]
    return d, xs



ENGS = ("te", "ve", "sc", "gp", "sy")
SEMS = ("s_te", "s_ve", "s_sc", "s_gp", "s_sy", "d_gp", "d_sy", "d_sc",
        "d_w0", "d_w1")


class K:
    def __init__(self, nc):
        self.nc = nc
        self.streams = {e: [] for e in ENGS}
        self.cnt = {s: 0 for s in SEMS}
        self.lastw = {}
        self.semh = {}
        self.scope = None

    def set_scope(self, name):
        self.scope = name

    def emit(self, eng, fn):
        fn._scope = self.scope
        self.streams[eng].append(fn)

    def wait(self, eng, sem, val):
        if val <= 0:
            return
        assert val <= self.cnt[sem], (eng, sem, val, self.cnt[sem])
        if self.lastw.get((eng, sem), 0) >= val:
            return
        self.lastw[(eng, sem)] = val

        def f(e, s=sem, v=val):
            e.wait_ge(self.semh[s], v)
        self.emit(eng, f)

    CHAIN = {"ve": "s_ve", "sc": "s_sc", "gp": "s_gp"}

    def op(self, eng, fname, *args, waits=(), inc=None, **kw):
        chain = self.CHAIN.get(eng) if fname != "dma_start" else None
        for s, v in waits:
            self.wait(eng, s, v)
        if chain is not None:
            self.wait(eng, chain, self.cnt[chain])
            if inc is None:
                inc = (chain, 1)
        tok = None
        if inc is not None:
            if isinstance(inc, str):
                inc = (inc, 1)
            assert chain is None or inc[0] == chain, (eng, fname, inc)
            self.cnt[inc[0]] += inc[1]
            tok = self.cnt[inc[0]]
            inc = (inc[0], inc[1], tok)

        def f(e, fname=fname, args=args, kw=kw, inc=inc):
            ins = getattr(e, fname)(*args, **kw)
            if inc is not None:
                ins.then_inc(self.semh[inc[0]], inc[1])
        self.emit(eng, f)
        return tok

    def dma(self, q, out, in_, waits=(), sem=None):
        if sem is None:
            sem = {"gp": "d_gp", "sy": "d_sy", "sc": "d_sc"}[q]
        self.op(q, "dma_start", waits=waits, inc=(sem, 16), out=out, in_=in_)
        return sem, self.cnt[sem]

    def barrier(self, engs=ENGS):
        for e in engs:
            if e == "gp":
                self.wait("gp", "d_gp", self.cnt["d_gp"])
                self.wait("gp", "d_w0", self.cnt["d_w0"])
                self.wait("gp", "d_w1", self.cnt["d_w1"])
            if e == "sy":
                self.wait("sy", "d_sy", self.cnt["d_sy"])
            if e == "sc":
                self.wait("sc", "d_sc", self.cnt["d_sc"])
        tgt = {}
        for e in engs:
            tgt[e] = self.op(e, "drain", inc="s_" + e)
        for e in engs:
            for o in engs:
                if o != e:
                    self.wait(e, "s_" + o, tgt[o])

    def build_block(self):
        nc = self.nc
        import contextlib
        stack = contextlib.ExitStack()
        for s in SEMS:
            self.semh[s] = stack.enter_context(nc.semaphore(s))
        def run_stream(eng_obj, fns):
            cur = None
            cur_id = None
            for f in fns:
                sc = getattr(f, "_scope", None)
                if sc != cur:
                    if cur is not None:
                        nc.leave_named_scope(cur, cur_id, False)
                    if sc is not None:
                        cur_id, _ = nc.enter_named_scope(sc, False)
                    cur = sc
                f(eng_obj)
            if cur is not None:
                nc.leave_named_scope(cur, cur_id, False)

        with stack:
            with nc.Block() as block:
                st = self.streams

                @block.tensor
                def _(t):
                    run_stream(t, st["te"])

                @block.vector
                def _(v):
                    run_stream(v, st["ve"])

                @block.scalar
                def _(s):
                    run_stream(s, st["sc"])

                @block.gpsimd
                def _(g):
                    run_stream(g, st["gp"])

                @block.sync
                def _(s):
                    run_stream(s, st["sy"])
        return nc



SHARED_SPECS = {}


def _declare_dram(nc, shared_shapes):
    t = {}
    for name, (shape, dt) in shared_shapes.items():
        t[name] = nc.dram_tensor(name, list(shape), dt, kind="ExternalInput").ap()
    return t


class Buffers:
    pass


def alloc_buffers(nc):
    B = Buffers()
    a = nc.alloc_sbuf_tensor
    B.y = [a(f"y{g}", [128, 2049], F32).ap() for g in range(NG)]
    B.poolA = a("poolA", [128, 16384], BF16).ap()
    B.poolB = a("poolB", [128, 16384], BF16).ap()
    B.poolC = a("poolC", [128, 16384], BF16).ap()
    B.poolF = a("poolF", [128, 4096], F32).ap()
    B.tabs = a("tabs", [128, 4096], BF16).ap()
    B.scr = a("scr", [128, 2816], F32).ap()
    B.col = a("colscr", [128, 64], F32).ap()
    B.rows16 = a("rows16", [16, 2048], BF16).ap()
    B.ident = a("c_ident", [128, 128], BF16).ap()
    B.rot = a("c_rot", [128, 128], BF16).ap()
    B.seg = a("c_seg", [128, 128], BF16).ap()
    B.exp16 = a("c_exp16", [16, 1024], BF16).ap()
    B.ones_f = a("c_ones_f", [128, 128], F32).ap()
    B.ones_b = a("c_ones_b", [128, 128], BF16).ap()
    B.onescol_b = a("c_onescol_b", [128, 1], BF16).ap()
    B.cmask = a("c_cmask", [128, 384], BF16).ap()
    B.w0m0 = a("c_w0m0", [128, 128], BF16).ap()
    B.w0m1 = a("c_w0m1", [128, 256], BF16).ap()
    B.mwp = a("c_mwp", [128, 256], BF16).ap()
    B.v_abq = a("v_abq", [128, 8], F32).ap()
    B.v_abk = a("v_abk", [128, 8], F32).ap()
    B.v_abv = a("v_abv", [128, 8], F32).ap()
    B.v_abo = a("v_abo", [128, 8], F32).ap()
    B.v_abqs = a("v_abqs", [128, 8], F32).ap()
    B.v_abks = a("v_abks", [128, 8], F32).ap()
    B.v_qs = a("v_qs", [128, 6], F32).ap()
    B.v_ks = a("v_ks", [128, 6], F32).ap()
    B.v_g = a("v_g", [128, 48], F32).ap()
    B.v_b = a("v_b", [128, 48], F32).ap()
    B.v_b1a = a("v_b1a", [128, 22], F32).ap()
    B.v_b1g = a("v_b1g", [128, 22], F32).ap()
    B.v_b2 = a("v_b2", [128, 48], F32).ap()
    B.v_wib = a("v_wib", [128, 32], F32).ap()
    B.v_wob = a("v_wob", [128, 8], F32).ap()
    B.v_gbc = a("v_gbc", [128, 8], F32).ap()
    B.eps5 = a("v_eps5", [128, 1], F32).ap()
    B.PSA = nc.alloc_psum_tensor("PSA", [128, 2048], F32).ap()
    B.PSB = nc.alloc_psum_tensor("PSB", [128, 1024], F32).ap()
    B.PSC = nc.alloc_psum_tensor("PSC", [128, 512], F32).ap()
    B.PST = nc.alloc_psum_tensor("PST", [128, 1024], BF16).ap()
    return B


def load_statics(K, B, t):
    pairs = [
        (B.ident, "ident"), (B.rot, "rot"), (B.seg, "seg"), (B.exp16, "exp16"),
        (B.ones_f, "ones_f"), (B.ones_b, "ones_b"), (B.onescol_b, "onescol_b"),
        (B.cmask, "cmask"), (B.w0m0, "w0m0"), (B.w0m1, "w0m1"),
        (B.mwp, "mwpair"),
        (B.v_abq, "a_bq"), (B.v_abk, "a_bk"), (B.v_abv, "a_bv"),
        (B.v_abo, "a_bo"),
        (B.v_qs, "f_qs"), (B.v_ks, "f_ks"), (B.v_g, "f_g"), (B.v_b, "f_b"),
        (B.v_b2, "f_b2"),
        (B.v_wib, "m_wib"), (B.v_wob, "m_wob"), (B.v_gbc, "m_gbc"),
    ]
    for dst, name in pairs:
        K.dma("sy", dst, t[name])
    tok = K.cnt["d_sy"]
    K.op("ve", "tensor_scalar_mul", B.v_abqs, B.v_abq, 0.125,
         waits=[("d_sy", tok)])
    K.op("ve", "tensor_scalar_mul", B.v_abks, B.v_abk, 0.125, inc="s_ve")
    K.op("ve", "memset", B.eps5, 1e-5)
    return tok


def proj_fmajor(K, B, wdram, nm, rhs_fn, evac_fn, t_tiles, wbuf0, wbuf1,
                nk=8, mc=128):
    wbufs = [wbuf0, wbuf1]
    wtok = [("d_w0", 0), ("d_w1", 0)]
    te0 = K.cnt["s_te"]
    wfree = [te0, te0]
    banks = [(B.PSA[:, 512 * i:512 * i + 512]) for i in range(4)]
    bfree = [K.cnt["s_ve"]] * 4
    bsem = ["s_ve"] * 4
    sc0 = K.cnt["s_sc"]
    bi = 0
    for m in range(nm):
        wb = m % 2
        K.wait("gp", "s_te", wfree[wb])
        wtok[wb] = K.dma("gp", wbufs[wb],
                         wdram[:, m * nk * mc:(m + 1) * nk * mc],
                         sem=f"d_w{wb}")
        for tt, tw in t_tiles:
            ps = banks[bi % 4][:, 0:tw]
            K.wait("te", *wtok[wb])
            K.wait("te", bsem[bi % 4], bfree[bi % 4])
            if bi < 4:
                K.wait("te", "s_sc", sc0)
            for kc in range(nk):
                K.op("te", "matmul", ps, wbufs[wb][:, kc * mc:(kc + 1) * mc],
                     rhs_fn(kc, tt), start=(kc == 0), stop=(kc == nk - 1),
                     inc=("s_te" if kc == nk - 1 else None))
            tetok = K.cnt["s_te"]
            if tt == t_tiles[-1][0]:
                wfree[wb] = tetok
            sem, ftok = evac_fn(m, tt, ps, tetok)
            bsem[bi % 4], bfree[bi % 4] = sem, ftok
            bi += 1


def stream_x_tile(K, B, xsrc, s, tt, tw=512):
    stg = [B.poolF[:, g * 512:g * 512 + tw] for g in range(NG)]
    for g in range(NG):
        K.dma("sy", stg[g], xsrc[s, g * 128:(g + 1) * 128, tt * 512:tt * 512 + tw])
    return stg, K.cnt["d_sy"]


def emit_ln_stats(K, B, srcs, tw, sq_bf, psA, psB, waits=()):
    is_bf = srcs[0].dtype == BF16
    for g in range(NG):
        K.op("te", "matmul", psA[:, 0:tw],
             B.ones_b if is_bf else B.ones_f, srcs[g],
             start=(g == 0), stop=(g == NG - 1),
             waits=(list(waits) if g == 0 else []),
             inc=("s_te" if g == NG - 1 else None))
    for g in range(NG):
        w = list(waits) if g == 0 else [("s_te", K.cnt["s_te"])]
        sctok = K.op("sc", "activation", sq_bf[:, 0:tw], srcs[g], AF.Square,
                     waits=w, inc="s_sc")
        K.op("te", "matmul", psB[:, 0:tw], B.ones_b, sq_bf[:, 0:tw],
             start=(g == 0), stop=(g == NG - 1), waits=[("s_sc", sctok)],
             inc="s_te")
    return K.cnt["s_te"]


def emit_ln_factors(K, B, psA, psB, tw, tetok, eps=1e-5):
    t1 = B.scr[:, 0:tw]
    t2 = B.scr[:, 512:512 + tw]
    K.op("ve", "tensor_scalar_mul", t1, psA[:, 0:tw], 1.0 / D,
         waits=[("s_te", tetok)])
    vtok = K.op("ve", "tensor_scalar_mul", t2, psB[:, 0:tw], 1.0 / D,
                inc="s_ve")
    stok = K.op("sc", "activation", psA[:, 0:tw], t1, AF.Square,
                waits=[("s_ve", vtok)], inc="s_sc")
    vtok = K.op("ve", "tensor_sub", t2, t2, psA[:, 0:tw],
                waits=[("s_sc", stok)], inc="s_ve")
    stok = K.op("sc", "activation", psB[:, 0:tw], t2, AF.Sqrt,
                bias=B.eps5[:, 0:1],
                waits=[("s_ve", vtok)], inc="s_sc")
    K.op("ve", "reciprocal", t2, psB[:, 0:tw],
         waits=[("s_sc", stok)])
    vtok = K.op("ve", "tensor_mul", t1, t1, t2, inc="s_ve")
    return t2, t1, vtok



def emit_preln(K, B, xsrc, s, dst_fn, out_f32=None, bank_guard=None):
    sq = B.scr.bitcast(BF16)[:, 2048:2560]
    psA = B.PSA[:, 0:512]
    psB = B.PSA[:, 512:1024]
    vfree = 0
    for tt in range(4):
        stg = [B.poolF[:, g * 512:(g + 1) * 512] for g in range(NG)]
        for g in range(NG):
            K.wait("sy", "s_ve", vfree)
            if bank_guard and tt == 0 and g == 0:
                K.wait("sy", *bank_guard)
            K.dma("sy", stg[g], xsrc[s, g * 128:(g + 1) * 128,
                                     tt * 512:(tt + 1) * 512])
        dtok = K.cnt["d_sy"]
        tetok = emit_ln_stats(K, B, stg, 512, sq, psA, psB,
                              waits=[("d_sy", dtok), ("s_ve", vfree)])
        a, c, vtok = emit_ln_factors(K, B, psA, psB, 512, tetok)
        for g in range(NG):
            K.op("ve", "tensor_mul", stg[g], stg[g], a)
            dst = dst_fn(g, tt) if out_f32 is None else out_f32(g, tt)
            vfree = K.op("ve", "tensor_sub", dst, stg[g], c, inc="s_ve")


def emit_postln_residual(K, B, fsrc_fn, xsrc, xdst, s, waits=(), tts=(0, 1, 2, 3)):
    sq = B.scr.bitcast(BF16)[:, 2048:2560]
    psA = B.PSA[:, 0:512]
    psB = B.PSA[:, 512:1024]
    tmpf = B.scr[:, 1280:1792]
    vfree = 0
    for tt in tts:
        stg = [B.poolF[:, g * 512:(g + 1) * 512] for g in range(NG)]
        K.wait("sy", "d_sy", K.cnt["d_sy"])
        for g in range(NG):
            K.wait("sy", "s_ve", vfree)
            K.dma("sy", stg[g], xsrc[s, g * 128:(g + 1) * 128,
                                     tt * 512:(tt + 1) * 512])
        dtok = K.cnt["d_sy"]
        fsrcs = [fsrc_fn(g, tt) for g in range(NG)]
        tetok = emit_ln_stats(K, B, fsrcs, 512, sq, psA, psB,
                              waits=list(waits) + [("s_ve", vfree)])
        a, c, vtok = emit_ln_factors(K, B, psA, psB, 512, tetok)
        for g in range(NG):
            K.op("ve", "tensor_mul", tmpf, fsrcs[g], a,
                 waits=[("d_sy", dtok)] if g == 0 else [])
            K.op("ve", "tensor_sub", tmpf, tmpf, c)
            vfree = K.op("ve", "tensor_add", stg[g], stg[g], tmpf, inc="s_ve")
            K.dma("sy", xdst[s, g * 128:(g + 1) * 128,
                             tt * 512:(tt + 1) * 512], stg[g],
                  waits=[("s_ve", vfree)])
    return K.cnt["d_sy"]





def proj_tmajor(K, B, wdram, lhs_fn, evac_fn, nk=8, tcs_all=None,
                ve_guard=None, co_fn=None):
    wv = [B.scr.bitcast(BF16)[:, 3584:4096], B.scr.bitcast(BF16)[:, 4096:4608]]
    wvtok = [("d_w0", 0), ("d_w1", 0)]
    wvfree = [K.cnt["s_te"]] * 2
    sc0 = K.cnt["s_sc"]
    ve0 = K.cnt["s_ve"] if ve_guard is None else ve_guard
    banks6 = [B.PSA[:, 512 * i:512 * (i + 1)] for i in range(4)] + \
             [B.PSB[:, 0:512], B.PSB[:, 512:1024]]
    bfree = [("s_sc", sc0)] * 6
    K.wait("te", "s_ve", ve0)
    K.wait("gp", "s_ve", ve0)
    K.wait("gp", "s_sc", sc0)
    if tcs_all is None:
        tcs_all = list(range(16))
    groups = [tcs_all[i:i + 6] for i in range(0, len(tcs_all), 6)]
    for tcs in groups:
        for sl in range(2):
            gtok = [0] * len(tcs)
            for kc in range(nk):
                wbi = kc % 2
                K.wait("gp", "s_te", wvfree[wbi])
                wvtok[wbi] = K.dma("gp", wv[wbi],
                                   wdram[:, (sl * nk + kc) * 512:
                                         (sl * nk + kc + 1) * 512],
                                   sem=f"d_w{wbi}")
                for j, tc in enumerate(tcs):
                    ps = banks6[j]
                    K.wait("te", *wvtok[wbi])
                    if kc == 0:
                        K.wait("te", *bfree[j])
                    last = (j == len(tcs) - 1) or kc == nk - 1
                    K.op("te", "matmul", ps,
                         lhs_fn(kc)[:, tc * 128:(tc + 1) * 128], wv[wbi],
                         start=(kc == 0), stop=(kc == nk - 1),
                         inc=("s_te" if last else None))
                    if kc == nk - 1:
                        gtok[j] = K.cnt["s_te"]
                wvfree[wbi] = K.cnt["s_te"]
            for j, tc in enumerate(tcs):
                bfree[j] = evac_fn(tc, sl, banks6[j], gtok[j])
                if co_fn is not None:
                    co_fn()


def elu_evac(K, B, dst, ps, tetok):
    t1 = B.scr[:, 0:512][:, 0:dst.shape[1]]
    t2 = B.scr[:, 512:1024][:, 0:dst.shape[1]]
    te = B.scr[:, 1024:1536][:, 0:dst.shape[1]]
    vtok = K.op("ve", "tensor_scalar_min", t1, ps, 0.0,
                waits=[("s_te", tetok)])
    stok = K.op("sc", "activation", te, t1, AF.Exp, waits=[("s_ve", vtok)],
                inc="s_sc")
    vtok2 = K.op("ve", "tensor_scalar_max", t2, ps, 0.0, inc="s_ve")
    K.op("ve", "tensor_add", dst, te, t2, waits=[("s_sc", stok)])
    return "s_ve", vtok2


def emit_l0_attn(K, B, t, s, xsrc, xdst):
    base = K.scope
    nc = K.nc
    ybf = [B.y[g].bitcast(BF16) for g in range(NG)]
    xln = lambda g, tt: ybf[g][:, tt * 512:(tt + 1) * 512]
    xln_g = lambda g: ybf[g][:, 0:2048]
    ao2 = lambda g, tt: ybf[g][:, 2048 + tt * 512:2048 + (tt + 1) * 512]
    qb = [B.poolA[:, g * 2048:(g + 1) * 2048] for g in range(NG)]
    kb = [B.poolB[:, g * 2048:(g + 1) * 2048] for g in range(NG)]
    vt = [B.poolC[:, c * 1024:(c + 1) * 1024] for c in range(16)]
    wq = B.scr.bitcast(BF16)[:, 0:1024]
    wq2 = B.scr.bitcast(BF16)[:, 1024:2048]
    t_tiles = [(i, 512) for i in range(4)]

    K.set_scope(f"{base}.preln")
    emit_preln(K, B, xsrc, s, xln)
    K.barrier()
    K.set_scope(f"{base}.proj")

    def ev_q(buf, bias, scale):
        def f(m, tt, ps, tetok):
            tok = K.op("sc", "activation", buf[m][:, tt * 512:(tt + 1) * 512],
                       ps, AF.Identity, scale=scale, bias=bias[:, m:m + 1],
                       waits=[("s_te", tetok)], inc="s_sc")
            return "s_sc", tok
        return f
    rhs = lambda kc, tt: xln(kc, tt)
    proj_fmajor(K, B, t["a_wq"], 8, rhs, ev_q(qb, B.v_abqs, 0.125), t_tiles,
                wq, wq2)
    proj_fmajor(K, B, t["a_wk"], 8, rhs, ev_q(kb, B.v_abk, 1.0), t_tiles,
                wq, wq2)

    def ev_v(tc, sl, ps, gtok):
        return "s_sc", K.op("sc", "activation", vt[tc][:, sl * 512:(sl + 1) * 512],
                            ps, AF.Copy, waits=[("s_te", gtok)], inc="s_sc")
    proj_tmajor(K, B, t["a_wv"], xln_g, ev_v)
    K.set_scope(f"{base}.attn")
    K.barrier()
    return qb, kb, vt, xln_g, ao2


def emit_l0_attn_attention(K, B, qb, kb, vt):
    Pb = B.poolF.bitcast(BF16)
    C = B.col
    te0, ve0, sc0 = K.cnt["s_te"], K.cnt["s_ve"], K.cnt["s_sc"]

    units0 = [(g, h) for g in range(NG) for h in range(2)]
    exp_tok = [sc0] * len(units0)
    st_tok = [te0] * len(units0)
    mx_tok = [ve0] * len(units0)
    nm_tok = [ve0] * len(units0)
    tt_tok = [te0] * len(units0)
    ev_tok = [sc0] * len(units0)
    pv_tok = [te0] * len(units0)
    ot_tok = [sc0] * len(units0)

    def b0_front_te(u):
        g, h = units0[u]
        hb = 64 * h
        if u >= 1:
            K.wait("te", "s_sc", exp_tok[u - 1])
        tok = 0
        for c in range(4):
            tok = K.op("te", "matmul", B.PSA[0:64, 512 * c:512 * (c + 1)],
                       qb[g][hb:hb + 64, 0:64],
                       kb[g][hb:hb + 64, 512 * c:512 * (c + 1)],
                       start=True, stop=True,
                       inc=("s_te" if c == 3 else None))
        st_tok[u] = tok

    def b0_front_rest(u):
        g, h = units0[u]
        cb = 8 + 16 * (u % 2)
        mx_tok[u] = K.op("ve", "reduce_max", C[0:64, cb:cb + 1], B.PSA[0:64, 0:2048],
                         axis=AX.X, negate=True, waits=[("s_te", st_tok[u])],
                         inc="s_ve")
        pb0 = 4096 + 2048 * (u % 2)
        tok = 0
        for c in range(4):
            w_ = [("s_ve", mx_tok[u])]
            if u >= 2:
                w_.append(("s_te", tt_tok[u - 2]))
            tok = K.op("sc", "activation", Pb[0:64, pb0 + 512 * c:pb0 + 512 * (c + 1)],
                       B.PSA[0:64, 512 * c:512 * (c + 1)], AF.Exp,
                       bias=C[0:64, cb:cb + 1], accum_out=C[0:64, cb + 1 + c:cb + 2 + c],
                       waits=w_, inc="s_sc")
        exp_tok[u] = tok

    def b0_back(u):
        g, h = units0[u]
        hb = 64 * h
        vcol = g * 128 + hb
        cb = 8 + 16 * (u % 2)
        K.op("ve", "tensor_add", C[0:64, cb + 5:cb + 6], C[0:64, cb + 1:cb + 2], C[0:64, cb + 2:cb + 3],
             waits=[("s_sc", exp_tok[u])])
        K.op("ve", "tensor_add", C[0:64, cb + 6:cb + 7], C[0:64, cb + 3:cb + 4], C[0:64, cb + 4:cb + 5])
        K.op("ve", "tensor_add", C[0:64, cb + 5:cb + 6], C[0:64, cb + 5:cb + 6], C[0:64, cb + 6:cb + 7])
        pb0 = 4096 + 2048 * (u % 2)
        pt0 = 1024 * (u % 2)
        K.op("ve", "reciprocal", C[0:64, cb + 5:cb + 6], C[0:64, cb + 5:cb + 6])
        nm_tok[u] = K.op("ve", "tensor_scalar_mul", Pb[0:64, pb0:pb0 + 2048],
                         Pb[0:64, pb0:pb0 + 2048], C[0:64, cb + 5:cb + 6], inc="s_ve")
        K.wait("te", "s_sc", ev_tok[u - 1] if u >= 1 else sc0)
        tok = 0
        for c in range(16):
            tok = K.op("te", "transpose", B.PST[:, 64 * c:64 * (c + 1)],
                       Pb[0:64, pb0 + 128 * c:pb0 + 128 * (c + 1)],
                       B.ident[0:64, 0:64],
                       waits=[("s_ve", nm_tok[u])] if c == 0 else [],
                       inc=("s_te" if c == 15 else None))
        tt_tok[u] = tok
        w_ = [("s_te", tt_tok[u])]
        if u >= 1:
            w_.append(("s_te", pv_tok[u - 1]))
        ev_tok[u] = K.op("sc", "activation", Pb[:, pt0:pt0 + 1024], B.PST,
                         AF.Copy, waits=w_, inc="s_sc")
        if u >= 1:
            K.wait("te", "s_sc", ot_tok[u - 1])
        tok = 0
        for c in range(16):
            tok = K.op("te", "matmul", B.PSC[0:64, 0:64],
                       vt[c][:, vcol:vcol + 64],
                       Pb[:, pt0 + 64 * c:pt0 + 64 * (c + 1)],
                       start=(c == 0), stop=(c == 15),
                       waits=[("s_sc", ev_tok[u])] if c == 0 else [],
                       inc=("s_te" if c == 15 else None))
        pv_tok[u] = tok
        ot_tok[u] = K.op("sc", "activation", qb[g][hb:hb + 64, 0:64],
                         B.PSC[0:64, 0:64], AF.Identity,
                         bias=B.v_abv[hb:hb + 64, g:g + 1],
                         waits=[("s_te", pv_tok[u])], inc="s_sc")

    for u in range(len(units0)):
        b0_front_te(u)
        if u >= 1:
            b0_back(u - 1)
        b0_front_rest(u)
    b0_back(len(units0) - 1)
    K.barrier(("te", "ve", "sc"))

    te0, ve0, sc0 = K.cnt["s_te"], K.cnt["s_ve"], K.cnt["s_sc"]
    units = [(g, h, a) for g in range(NG) for h in range(2)
             for a in list(range(1, 30, 2)) + [31]]
    n = len(units)
    exp_tok = [sc0] * n
    stok = [te0] * n
    mxt = [ve0] * n
    nmt = [ve0] * n
    ttt = [te0] * n
    evt = [sc0] * n
    pvt = [te0] * n
    ott = [sc0] * n
    DEPTH = 4

    def geom(u):
        g, h, a = units[u]
        solo = (a == 31)
        kst = 64 * (a - 1)
        kw = 128 if solo else 256
        qw = 64 if solo else 128
        return g, h, 64 * h, a, solo, kst, kw, qw

    PSTs = [B.PST, B.PSB.bitcast(BF16)[:, 0:1024]]
    PSCs = [B.PSC[0:64, 0:256], B.PSB[0:64, 512:768]]

    def w_front_te(u):
        g, h, hb, a, solo, kst, kw, qw = geom(u)
        s = u % DEPTH
        ps = B.PSA[:, 512 * s:512 * s + kw]
        if u >= DEPTH:
            K.wait("te", "s_sc", exp_tok[u - DEPTH])
        stok[u] = K.op("te", "matmul", ps[0:qw, :],
                       qb[g][hb:hb + 64, 64 * a:64 * a + qw],
                       kb[g][hb:hb + 64, kst:kst + kw],
                       start=True, stop=solo,
                       inc=("s_te" if solo else None))
        if not solo:
            stok[u] = K.op("te", "matmul", ps, B.ident, B.mwp,
                           start=False, stop=True, inc="s_te")

    def w_front_rest(u):
        g, h, hb, a, solo, kst, kw, qw = geom(u)
        s = u % DEPTH
        ps = B.PSA[:, 512 * s:512 * s + kw]
        mxt[u] = K.op("ve", "reduce_max", C[0:qw, s:s + 1], ps[0:qw, :],
                      axis=AX.X, negate=True, waits=[("s_te", stok[u])],
                      inc="s_ve")
        Pp = Pb[0:qw, 1024 * s:1024 * s + kw]
        w_ = [("s_ve", mxt[u])]
        if u >= DEPTH:
            w_.append(("s_te", ttt[u - DEPTH]))
        exp_tok[u] = K.op("sc", "activation", Pp, ps[0:qw, :], AF.Exp,
                          bias=C[0:qw, s:s + 1], accum_out=C[0:qw, 4 + s:5 + s],
                          waits=w_, inc="s_sc")

    def w_mid(u):
        g, h, hb, a, solo, kst, kw, qw = geom(u)
        s = u % DEPTH
        Pp = Pb[0:qw, 1024 * s:1024 * s + kw]
        K.op("ve", "reciprocal", C[0:qw, 4 + s:5 + s], C[0:qw, 4 + s:5 + s],
             waits=[("s_sc", exp_tok[u])])
        nmt[u] = K.op("ve", "tensor_scalar_mul", Pp, Pp, C[0:qw, 4 + s:5 + s],
                      inc="s_ve")
        nb = kw // 128
        pst = PSTs[u % 2]
        K.wait("te", "s_sc", evt[u - 2] if u >= 2 else sc0)
        tok = 0
        for c in range(nb):
            tok = K.op("te", "transpose",
                       pst[:, 128 * c:128 * c + qw],
                       Pb[0:qw, 1024 * s + 128 * c:1024 * s + 128 * (c + 1)],
                       B.ident[0:qw, 0:qw],
                       waits=[("s_ve", nmt[u])] if c == 0 else [],
                       inc=("s_te" if c == nb - 1 else None))
        ttt[u] = tok
        w_ = [("s_te", ttt[u])]
        if u >= 2:
            w_.append(("s_te", pvt[u - 2]))
        wpt = 128 * (nb - 1) + qw
        PT = Pb[:, 4096 + 512 * (u % 2):4096 + 512 * (u % 2) + wpt]
        evt[u] = K.op("sc", "activation", PT, pst[:, 0:wpt], AF.Copy,
                      waits=w_, inc="s_sc")
        c0 = (a - 1) // 2
        vcol = g * 128 + hb
        psc = PSCs[u % 2]
        if u >= 2:
            K.wait("te", "s_sc", ott[u - 2])
        tok = 0
        for c in range(nb):
            tok = K.op("te", "matmul", psc[:, 0:qw],
                       vt[c0 + c][:, vcol:vcol + 64],
                       PT[:, 128 * c:128 * c + qw],
                       start=(c == 0), stop=(c == nb - 1),
                       waits=[("s_sc", evt[u])] if c == 0 else [],
                       inc=("s_te" if c == nb - 1 else None))
        pvt[u] = tok

    def w_tail(u):
        g, h, hb, a, solo, kst, kw, qw = geom(u)
        ott[u] = K.op("sc", "activation", qb[g][hb:hb + 64, 64 * a:64 * a + qw],
                      PSCs[u % 2][:, 0:qw], AF.Identity,
                      bias=B.v_abv[hb:hb + 64, g:g + 1],
                      waits=[("s_te", pvt[u])], inc="s_sc")

    for u in range(n):
        w_front_te(u)
        if u >= 1:
            w_mid(u - 1)
        if u >= 2:
            w_tail(u - 2)
        w_front_rest(u)
    w_mid(n - 1)
    w_tail(n - 2)
    w_tail(n - 1)


def emit_l0a(K, B, t, s, xsrc, xdst):
    base = K.scope
    qb, kb, vt, xln_g, ao2 = emit_l0_attn(K, B, t, s, xsrc, xdst)
    emit_l0_attn_attention(K, B, qb, kb, vt)
    K.barrier()
    K.set_scope(f"{base}.wo")
    wq = B.scr.bitcast(BF16)[:, 0:1024]
    wq2 = B.scr.bitcast(BF16)[:, 1024:2048]

    def ev_o(m, tt, ps, tetok):
        tok = K.op("sc", "activation", ao2_list[m][tt], ps, AF.Identity,
                   bias=B.v_abo[:, m:m + 1], waits=[("s_te", tetok)],
                   inc="s_sc")
        return "s_sc", tok
    ao2_list = [[ao2(g, tt) for tt in range(4)] for g in range(NG)]
    rhs = lambda kc, tt: qb[kc][:, tt * 512:(tt + 1) * 512]
    proj_fmajor(K, B, t["a_wo"], 8, rhs, ev_o, [(i, 512) for i in range(4)],
                wq, wq2)
    K.barrier()
    K.set_scope(f"{base}.postln")
    emit_postln_residual(K, B, ao2, xsrc, xdst, s)
    K.barrier()
    K.set_scope(base)



def _np_dt(arr):
    return BF16 if arr.dtype == BF else F32


def build_program(shared, blocks, n_seq=2):
    nc = bass.Bass("TRN2", target_bir_lowering=False, debug=False)
    k = K(nc)
    t = {}
    for name, arr in shared.items():
        t[name] = nc.dram_tensor(name, list(arr.shape), _np_dt(arr),
                                 kind="ExternalInput").ap()
    xin = nc.dram_tensor("xin", [2, 1024, 2048], F32, kind="ExternalInput").ap()
    xout = nc.dram_tensor("xout", [2, 1024, 2048], F32,
                          kind="ExternalOutput").ap()
    chain = [xin]
    for i in range(len(blocks) - 1):
        chain.append(nc.dram_tensor(f"xw{i}", [2, 1024, 2048], F32,
                                    kind="Internal").ap())
    chain.append(xout)
    B = alloc_buffers(nc)
    load_statics(k, B, t)
    k.barrier()
    EM = {"l0a": emit_l0a, "l0f": emit_l0f, "l1a": emit_l1a, "l1f": emit_l1f}
    for s in range(n_seq):
        for bi, blk in enumerate(blocks):
            k.set_scope(f"s{s}.{blk}")
            EM[blk](k, B, t, s, chain[bi], chain[bi + 1])
    k.set_scope(None)
    k.barrier()
    k.build_block()
    return nc


def _run(shared, xs, blocks, cores=8, n_seq=2, **kw):
    nc = build_program(shared, blocks, n_seq=n_seq)
    in_maps = [dict(shared, xin=xs[c]) for c in range(cores)]
    res = run_bass_kernel_spmd(nc, in_maps, core_ids=list(range(cores)), **kw)
    return res


def kernel(**inputs):
    shared, xs = prep_inputs(inputs)
    res = _run(shared, xs, ["l0a", "l0f", "l1a", "l1f"])
    out = np.stack([np.asarray(r["xout"]) for r in res.results])
    out = out.reshape(16, 1024, 2048).transpose(0, 2, 1)
    return np.ascontiguousarray(out.astype(np.float32))


def emit_l0f(K, B, t, s, xsrc, xdst):
    raise NotImplementedError


def emit_l1a(K, B, t, s, xsrc, xdst):
    raise NotImplementedError


def emit_l1f(K, B, t, s, xsrc, xdst):
    raise NotImplementedError


def emit_l1a(K, B, t, s, xsrc, xdst):
    base = K.scope
    ybf = [B.y[g].bitcast(BF16) for g in range(NG)]
    xln = lambda g, tt: ybf[g][:, tt * 512:(tt + 1) * 512]
    xln_g = lambda g: ybf[g][:, 0:2048]
    ao2 = lambda g, tt: ybf[g][:, 2048 + tt * 512:2048 + (tt + 1) * 512]
    qel = [B.poolA[:, g * 2048:(g + 1) * 2048] for g in range(NG)]
    ktm = [B.poolB[:, c * 1024:(c + 1) * 1024] for c in range(16)]
    vtm = [B.poolC[:, c * 1024:(c + 1) * 1024] for c in range(16)]
    aob = [B.poolB[:, g * 2048:(g + 1) * 2048] for g in range(NG)]
    wq = B.scr.bitcast(BF16)[:, 3072:4096]
    wq2 = B.scr.bitcast(BF16)[:, 4096:5120]
    t_tiles = [(i, 512) for i in range(4)]

    K.set_scope(f"{base}.preln")
    emit_preln(K, B, xsrc, s, xln)
    K.barrier()
    K.set_scope(f"{base}.proj")

    def ev_q(m, tt, ps, tetok):
        return elu_evac(K, B, qel[m][:, tt * 512:(tt + 1) * 512], ps, tetok)
    rhs = lambda kc, tt: xln(kc, tt)
    proj_fmajor(K, B, t["b_wq"], 8, rhs, ev_q, t_tiles, wq, wq2)
    def ev_k(tc, sl, ps, gtok):
        return elu_evac(K, B, ktm[tc][:, sl * 512:(sl + 1) * 512], ps, gtok)
    proj_tmajor(K, B, t["b_wk"], xln_g, ev_k)

    def ev_v(tc, sl, ps, gtok):
        return "s_sc", K.op("sc", "activation",
                            vtm[tc][:, sl * 512:(sl + 1) * 512], ps, AF.Copy,
                            waits=[("s_te", gtok)], inc="s_sc")
    proj_tmajor(K, B, t["b_wv"], xln_g, ev_v)
    K.barrier()
    K.set_scope(f"{base}.kv")

    KVsb = B.scr.bitcast(BF16)[:, 2048:3072]
    ksd = B.scr.bitcast(BF16)[:, 0:2048]
    K.op("ve", "memset", KVsb, 0.0)
    kvf = [("s_ve", 0), ("s_ve", 0)]
    ksf = ("s_ve", 0)
    for g in range(NG):
        pkv = B.PSA[:, 512 * (g % 2):512 * (g % 2) + 128]
        K.wait("te", *kvf[g % 2])
        for tc in range(16):
            K.op("te", "matmul", pkv,
                 ktm[tc][:, 128 * g:128 * (g + 1)],
                 vtm[tc][:, 128 * g:128 * (g + 1)],
                 start=(tc == 0), stop=(tc == 15),
                 inc=("s_te" if tc == 15 else None))
        kvtok = K.cnt["s_te"]
        K.wait("te", *ksf)
        for tc in range(16):
            K.op("te", "matmul", B.PSC[:, 0:1],
                 ktm[tc][:, 128 * g:128 * (g + 1)], B.onescol_b,
                 start=(tc == 0), stop=(tc == 15),
                 inc=("s_te" if tc == 15 else None))
        kstok = K.cnt["s_te"]
        for hb in (0, 64):
            tok = K.op("ve", "tensor_copy",
                       KVsb[hb:hb + 64, 128 * g + hb:128 * g + hb + 64],
                       pkv[hb:hb + 64, hb:hb + 64],
                       waits=[("s_te", kvtok)], inc="s_ve")
        kvf[g % 2] = ("s_ve", tok)
        tok2 = K.op("ve", "tensor_tensor", ksd[:, 128 * g:128 * g + 16],
                    B.seg[:, 16 * g:16 * (g + 1)],
                    B.PSC[:, 0:1].broadcast_to((128, 16)), ALU.mult,
                    waits=[("s_te", kstok)], inc="s_ve")
        ksf = ("s_ve", tok2)
    K.barrier()

    K.set_scope(f"{base}.zout")
    psO = [B.PSA[:, 512 * i:512 * (i + 1)] for i in range(2)]
    psZB = [B.PSA[:, 1024 + 512 * i:1024 + 512 * (i + 1)] for i in range(2)]
    of = [("s_sc", 0), ("s_sc", 0)]
    zf = [("s_ve", 0), ("s_ve", 0)]
    for tt in range(4):
        psZ = B.PSB[0:16, 512:1024]
        K.wait("te", "s_ve", K.cnt["s_ve"])
        for g in range(NG):
            K.op("te", "matmul", psZ, ksd[:, 128 * g:128 * g + 16],
                 qel[g][:, tt * 512:(tt + 1) * 512],
                 start=(g == 0), stop=(g == NG - 1),
                 inc=("s_te" if g == NG - 1 else None))
        ztok = K.cnt["s_te"]
        K.op("ve", "tensor_scalar_add", psZ, psZ, 1e-6,
             waits=[("s_te", ztok)])
        zrt = B.scr[0:16, 2304:2816]
        K.op("ve", "reciprocal", zrt, psZ)
        rtok = K.op("ve", "tensor_copy",
                    B.rows16[0:16, 512 * tt:512 * (tt + 1)], zrt, inc="s_ve")
        for g in range(NG):
            pz = psZB[g % 2]
            K.wait("te", *zf[g % 2])
            K.op("te", "matmul", pz, B.exp16[:, 128 * g:128 * (g + 1)],
                 B.rows16[0:16, 512 * tt:512 * (tt + 1)],
                 start=True, stop=True,
                 waits=[("s_ve", rtok)], inc="s_te")
            qtok = K.op("ve", "tensor_tensor", qel[g][:, tt * 512:(tt + 1) * 512],
                        qel[g][:, tt * 512:(tt + 1) * 512], pz, ALU.mult,
                        waits=[("s_te", K.cnt["s_te"])], inc="s_ve")
            zf[g % 2] = ("s_ve", qtok)
            po = psO[g % 2]
            K.wait("te", *of[g % 2])
            K.op("te", "matmul", po, KVsb[:, 128 * g:128 * (g + 1)],
                 qel[g][:, tt * 512:(tt + 1) * 512],
                 start=True, stop=True, waits=[("s_ve", qtok)], inc="s_te")
            otok = K.op("sc", "activation", aob[g][:, tt * 512:(tt + 1) * 512],
                        po, AF.Copy, waits=[("s_te", K.cnt["s_te"])],
                        inc="s_sc")
            of[g % 2] = ("s_sc", otok)
    K.barrier()

    def ev_o(m, tt, ps, tetok):
        return "s_sc", K.op("sc", "activation", ao2(m, tt), ps, AF.Copy,
                            waits=[("s_te", tetok)], inc="s_sc")
    K.set_scope(f"{base}.wo")
    rhs2 = lambda kc, tt: aob[kc][:, tt * 512:(tt + 1) * 512]
    proj_fmajor(K, B, t["b_wo"], 8, rhs2, ev_o, t_tiles, wq, wq2)
    K.barrier()
    K.set_scope(f"{base}.postln")
    emit_postln_residual(K, B, ao2, xsrc, xdst, s)
    K.barrier()
    K.set_scope(base)


def emit_l1f(K, B, t, s, xsrc, xdst):
    ybf = [B.y[g].bitcast(BF16) for g in range(NG)]
    xln = lambda g, tt: ybf[g][:, tt * 512:(tt + 1) * 512]
    xln_g = lambda g: ybf[g][:, 0:2048]
    fout = lambda g, lt: ybf[g][:, 2048 + lt * 512:2048 + (lt + 1) * 512]
    maskb = ybf[0][:, 3072:3136]
    Usb = ybf[1][:, 3072:3328]
    UT = B.y[2].bitcast(BF16)[0:32, 3072:4096]
    scb = B.scr.bitcast(BF16)
    gbuf = scb[:, 5120:5184]
    labuf = scb[:, 4608:4864]
    base = K.scope
    lbbuf = [scb[0:32, 4864:4992], scb[0:32, 4992:5120]]
    K.set_scope(f"{base}.preln")
    emit_preln(K, B, xsrc, s, xln)
    K.dma("gp", gbuf, t["m_gate"])
    K.dma("gp", labuf, t["m_la"])
    K.barrier()
    inter = [(B.poolB[:, c * 1024:(c + 1) * 1024] if c < 16 else
              B.poolC[:, (c - 16) * 1024:(c - 15) * 1024]) for c in range(32)]
    wislab = [B.poolA[:, i * 1024:(i + 1) * 1024] for i in range(2)]
    woslab = [B.poolA[:, 2048 + i * 4096:2048 + (i + 1) * 4096] for i in range(2)]

    for hf in range(2):
        t0 = hf * 1024
        K.set_scope(f"{base}.h{hf}.gate")
        for j in range(8):
            tc = hf * 8 + j
            psZ = B.PSC[:, 0:8]
            psT = B.PSB[:, 0:32]
            K.wait("te", "s_ve", K.cnt["s_ve"])
            for kc in range(NG):
                K.op("te", "matmul", psZ,
                     xln_g(kc)[:, tc * 128:(tc + 1) * 128],
                     gbuf[:, kc * 8:(kc + 1) * 8],
                     start=(kc == 0), stop=(kc == NG - 1),
                     inc=("s_te" if kc == NG - 1 else None))
            ztok = K.cnt["s_te"]
            for kc in range(NG):
                K.op("te", "matmul", psT,
                     xln_g(kc)[:, tc * 128:(tc + 1) * 128],
                     labuf[:, kc * 32:(kc + 1) * 32],
                     start=(kc == 0), stop=(kc == NG - 1),
                     inc=("s_te" if kc == NG - 1 else None))
            ttok = K.cnt["s_te"]
            zf = B.scr[:, 0:8]
            m8 = B.scr[:, 8:9]
            K.op("ve", "tensor_add", zf, psZ, B.v_gbc, waits=[("s_te", ztok)])
            K.op("ve", "reduce_max", m8, zf, axis=AX.X)
            mk = maskb[:, j * 8:(j + 1) * 8]
            K.op("ve", "tensor_scalar", mk, zf, m8[:, 0:1], None, ALU.is_equal)
            uu = Usb[:, j * 32:(j + 1) * 32]
            vtok = K.op("ve", "tensor_tensor",
                        uu.rearrange("p (e r) -> p e r", r=4),
                        psT.rearrange("p (e r) -> p e r", r=4),
                        mk.unsqueeze(2).broadcast_to((128, 8, 4)), ALU.mult,
                        waits=[("s_te", ttok)], inc="s_ve")
            trtok = K.op("te", "transpose", B.PST[0:32, j * 128:(j + 1) * 128],
                         uu, B.ident, waits=[("s_ve", vtok)], inc="s_te")
        K.op("ve", "tensor_copy", UT, B.PST[0:32, 0:1024],
             waits=[("s_te", K.cnt["s_te"])], inc="s_ve")
        K.barrier(("te", "ve", "sc", "gp"))
        K.set_scope(f"{base}.h{hf}.p1")
        wtok = [("d_w0", 0), ("d_w1", 0)]
        wfree = [K.cnt["s_te"]] * 2
        banks = [B.PSA[:, 512 * i:512 * (i + 1)] for i in range(4)]
        bfree = [("s_sc", K.cnt["s_sc"])] * 4
        bi = 0
        for fc in range(32):
            wb = fc % 2
            K.wait("gp", "s_te", wfree[wb])
            K.dma("gp", wislab[wb], t["m_wi"][:, fc * 1024:(fc + 1) * 1024],
                  sem=f"d_w{wb}")
            wtok[wb] = K.dma("gp", lbbuf[wb],
                             t["m_lb"][0:32, fc * 128:(fc + 1) * 128],
                             sem=f"d_w{wb}")
            for tt2 in range(2):
                ps = banks[bi % 4]
                K.wait("te", *wtok[wb])
                K.wait("te", *bfree[bi % 4])
                for kc in range(NG):
                    K.op("te", "matmul", ps, wislab[wb][:, kc * 128:(kc + 1) * 128],
                         xln_g(kc)[:, t0 + tt2 * 512:t0 + (tt2 + 1) * 512],
                         start=(kc == 0), stop=False)
                K.op("te", "matmul", ps, lbbuf[wb],
                     UT[0:32, tt2 * 512:(tt2 + 1) * 512],
                     start=False, stop=True, inc="s_te")
                tetok = K.cnt["s_te"]
                if tt2 == 1:
                    wfree[wb] = tetok
                stok = K.op("sc", "activation",
                            inter[fc][:, tt2 * 512:(tt2 + 1) * 512], ps,
                            AF.Relu, bias=B.v_wib[:, fc:fc + 1],
                            waits=[("s_te", tetok)], inc="s_sc")
                bfree[bi % 4] = ("s_sc", stok)
                bi += 1
        K.barrier(("te", "ve", "sc", "gp"))
        K.set_scope(f"{base}.h{hf}.p2")
        wtok = [("d_w0", 0), ("d_w1", 0)]
        wfree = [K.cnt["s_te"]] * 2
        bfree = [("s_sc", K.cnt["s_sc"])] * 4
        bi = 0
        for m in range(8):
            wb = m % 2
            K.wait("gp", "s_te", wfree[wb])
            wtok[wb] = K.dma("gp", woslab[wb],
                             t["m_wo"][:, m * 4096:(m + 1) * 4096],
                             sem=f"d_w{wb}")
            for tt2 in range(2):
                ps = banks[bi % 4]
                K.wait("te", *wtok[wb])
                K.wait("te", *bfree[bi % 4])
                for fc in range(32):
                    K.op("te", "matmul", ps, woslab[wb][:, fc * 128:(fc + 1) * 128],
                         inter[fc][:, tt2 * 512:(tt2 + 1) * 512],
                         start=(fc == 0), stop=(fc == 31),
                         inc=("s_te" if fc == 31 else None))
                tetok = K.cnt["s_te"]
                if tt2 == 1:
                    wfree[wb] = tetok
                lt = hf * 2 + tt2
                stok = K.op("sc", "activation", fout(m, lt % 2), ps,
                            AF.Identity, bias=B.v_wob[:, m:m + 1],
                            waits=[("s_te", tetok)], inc="s_sc")
                bfree[bi % 4] = ("s_sc", stok)
                bi += 1
        K.barrier(("te", "ve", "sc", "gp", "sy"))
        K.set_scope(f"{base}.h{hf}.postln")
        emit_postln_residual(K, B, lambda g, tt: fout(g, tt - 2 * hf),
                             xsrc, xdst, s, tts=(2 * hf, 2 * hf + 1))
        K.barrier()
    K.set_scope(base)


def emit_l0f(K, B, t, s, xsrc, xdst):
    base = K.scope
    y = B.y
    yv = lambda g, tt: y[g][:, 1 + tt * 512:1 + (tt + 1) * 512]
    pfb = B.poolF.bitcast(BF16)
    qb = [B.poolA[:, g * 2048:(g + 1) * 2048] for g in range(NG)]
    kb = [B.poolB[:, g * 2048:(g + 1) * 2048] for g in range(NG)]
    vt = [B.poolC[:, c * 1024:(c + 1) * 1024] for c in range(16)]
    scb = B.scr.bitcast(BF16)
    K.set_scope(f"{base}.preln")
    for g in range(NG):
        K.op("ve", "memset", y[g][:, 0:1], 0.0)
    emit_preln(K, B, xsrc, s, None, out_f32=yv)
    K.barrier()

    for li in range(6):
        K.dma("sy", B.v_b1a, t["f_b1a"][:, 22 * li:22 * (li + 1)],
              waits=[("s_sc", K.cnt["s_sc"]), ("s_ve", K.cnt["s_ve"])])
        K.dma("sy", B.v_b1g, t["f_b1g"][:, 22 * li:22 * (li + 1)])
        for hf in range(2):
            t0 = hf * 1024
            K.set_scope(f"{base}.L{li}h{hf}.qkv")
            K.dma("sy", B.tabs[:, 0:1024], t["tab_ct"][:, t0:t0 + 1024],
                  waits=[("s_ve", K.cnt["s_ve"])])
            K.dma("sy", B.tabs[:, 1024:2048], t["tab_st"][:, t0:t0 + 1024])
            K.dma("sy", B.tabs[:, 2048:3072], t["tab_cti"][:, t0:t0 + 1024])
            K.dma("sy", B.tabs[:, 3072:4096], t["tab_sti"][:, t0:t0 + 1024])
            ttabs = ("d_sy", K.cnt["d_sy"])
            ysh = [pfb[:, g * 1024:(g + 1) * 1024] for g in range(NG)]
            shtok = 0
            for g in range(NG):
                off = (1 + t0) if g < 4 else t0
                shtok = K.op("gp", "tensor_copy", ysh[g],
                             y[g][:, off:off + 1024], inc="s_gp")
            sq2 = [scb[:, 2048:2560], scb[:, 2560:3072]]
            psN = [B.PSB[0:16, 0:512], B.PSB[0:16, 512:1024]]
            nfree = [("s_ve", K.cnt["s_ve"]), ("s_ve", K.cnt["s_ve"])]
            sqi = [0]

            def ev_qk(buf, scol, ptab):
                def f(m, tt2, ps, tetok):
                    j = sqi[0] % 2
                    sqi[0] += 1
                    stok = K.op("sc", "activation", sq2[j], ps, AF.Square,
                                waits=[("s_te", tetok)], inc="s_sc")
                    if m == 0:
                        K.wait("te", *nfree[tt2])
                    K.op("te", "matmul", psN[tt2], B.seg[:, 16 * m:16 * (m + 1)],
                         sq2[j], start=(m == 0), stop=(m == NG - 1),
                         waits=[("s_sc", stok)], inc="s_te")
                    etok = K.op("sc", "activation",
                                buf[m][:, t0 + tt2 * 512:t0 + (tt2 + 1) * 512],
                                ps, AF.Copy, scale=scol,
                                waits=[], inc="s_sc")
                    return "s_sc", etok
                return f
            rhs_sh = lambda kc, tt2: ysh[kc][:, tt2 * 512:(tt2 + 1) * 512]
            wqb = [scb[:, 3584:4608], scb[:, 4608:5632]]
            proj_fmajor(K, B, t[f"f_wq{li}"], 8, rhs_sh,
                        ev_qk(qb, B.v_qs[:, li:li + 1], None),
                        [(0, 512), (1, 512)], wqb[0], wqb[1])
            zr = B.scr[0:16, 0:512]
            for tt2 in range(2):
                stok = K.op("sc", "activation", psN[tt2], psN[tt2], AF.Sqrt,
                            waits=[("s_te", K.cnt["s_te"])], inc="s_sc")
                K.op("ve", "tensor_scalar_max", psN[tt2], psN[tt2], 1e-12,
                     waits=[("s_sc", stok)])
                K.op("ve", "reciprocal", zr, psN[tt2])
                ntok = K.op("ve", "tensor_scalar_mul",
                            B.rows16[0:16, t0 + tt2 * 512:t0 + (tt2 + 1) * 512],
                            zr, 8.0, inc="s_ve")
                nfree[tt2] = ("s_ve", ntok)
            K.set_scope(f"{base}.L{li}h{hf}.kproj")
            sqi[0] = 0
            proj_fmajor(K, B, t[f"f_wk{li}"], 8, rhs_sh,
                        ev_qk(kb, B.v_ks[:, li:li + 1], None),
                        [(0, 512), (1, 512)], wqb[0], wqb[1])
            for tt2 in range(2):
                stok = K.op("sc", "activation", psN[tt2], psN[tt2], AF.Sqrt,
                            waits=[("s_te", K.cnt["s_te"])], inc="s_sc")
                K.op("ve", "tensor_scalar_max", psN[tt2], psN[tt2], 1e-12,
                     waits=[("s_sc", stok)])
                K.op("ve", "reciprocal", zr, psN[tt2])
                ntok = K.op("ve", "tensor_copy",
                            scb[0:16, 2048 + tt2 * 512:2048 + (tt2 + 1) * 512],
                            zr, inc="s_ve")
                nfree[tt2] = ("s_ve", ntok)
            ve_pre_rot = K.cnt["s_ve"]
            K.set_scope(f"{base}.L{li}h{hf}.vrot")
            rot = RotarySteps(K, B, ttabs)
            rot.add(qb, t0, 0, 1024, B.rows16[0:16, t0:t0 + 1024])
            rot.add(kb, t0, 2048, 3072, scb[0:16, 2048:3072])
            def ev_v(tcl, sl, ps, gtok2):
                tok = K.op("sc", "activation",
                           vt[hf * 8 + tcl][:, sl * 512:(sl + 1) * 512],
                           ps, AF.Copy, waits=[("s_te", gtok2)],
                           inc="s_sc")
                return "s_sc", tok
            proj_tmajor(K, B, t[f"f_wv{li}"], lambda kc: ysh[kc], ev_v,
                        tcs_all=list(range(8)), ve_guard=ve_pre_rot,
                        co_fn=lambda: rot.step(2))
            rot.step(len(rot.todo))
            K.barrier(("te", "ve", "sc", "gp", "sy"))
        K.set_scope(f"{base}.L{li}.attn")
        _l0f_attention(K, B, qb, kb, vt)
        K.barrier()
        K.set_scope(f"{base}.L{li}.wo")
        def ev_o(m, tt, ps, tetok):
            tok = K.op("ve", "tensor_tensor", yv(m, tt), ps, yv(m, tt),
                       ALU.add, waits=[("s_te", tetok)], inc="s_ve")
            return "s_ve", tok
        proj_fmajor(K, B, t[f"f_wo{li}"], 8,
                    lambda kc, tt: qb[kc][:, tt * 512:(tt + 1) * 512],
                    ev_o, [(i, 512) for i in range(4)],
                    scb[:, 3584:4608], scb[:, 4608:5632])
        K.barrier()
        for hf in range(2):
            t0 = hf * 1024
            K.set_scope(f"{base}.L{li}h{hf}.ffnln")
            xn = [pfb[:, g * 1024:(g + 1) * 1024] for g in range(NG)]
            sqb = B.scr.bitcast(BF16)[:, 2048:2560]
            psA = B.PSA[:, 0:512]
            psB = B.PSA[:, 512:1024]
            tmpf = B.scr[:, 1280:1792]
            for tt2 in range(2):
                srcs = [y[g][:, (1 + t0 + tt2 * 512 - (0 if g < 4 else 1)):
                             (1 + t0 + (tt2 + 1) * 512 - (0 if g < 4 else 1))]
                        for g in range(NG)]
                tetok = emit_ln_stats(K, B, srcs, 512, sqb, psA, psB,
                                      waits=[("s_ve", K.cnt["s_ve"]),
                                             ("s_sc", K.cnt["s_sc"])])
                a, c, vtok = emit_ln_factors(K, B, psA, psB, 512, tetok)
                tmpd = [tmpf, B.scr[:, 1792:2304]]
                sct = [K.cnt["s_sc"]] * 2
                for g in range(NG):
                    tf = tmpd[g % 2]
                    K.op("ve", "tensor_mul", tf, srcs[g], a,
                         waits=[("s_sc", sct[g % 2])])
                    vtok = K.op("ve", "tensor_sub", tf, tf, c, inc="s_ve")
                    sct[g % 2] = K.op("sc", "activation",
                         xn[g][:, tt2 * 512:(tt2 + 1) * 512], tf,
                         AF.Identity, scale=B.v_g[:, 8 * li + g:8 * li + g + 1],
                         bias=B.v_b[:, 8 * li + g:8 * li + g + 1],
                         waits=[("s_ve", vtok)], inc="s_sc")
            K.barrier(("te", "ve", "sc", "gp"))
            K.set_scope(f"{base}.L{li}h{hf}.w1")
            w2a = B.poolA[:, 4096:16384]
            w2c = B.poolC[:, 6144:16384]
            w2tok1 = K.dma("gp", w2a, t[f"f_w2{li}"][:, 0:12288])
            w2tok2 = K.dma("gp", w2c, t[f"f_w2{li}"][:, 12288:22528])

            def w2tile(m, kc):
                flat = m * 2816 + kc * 128
                return (w2a[:, flat:flat + 128] if flat < 12288 else
                        w2c[:, flat - 12288:flat - 12288 + 128])
            inter = [(B.poolB[:, cc * 1024:(cc + 1) * 1024] if cc < 16 else
                      B.poolC[:, (cc - 16) * 1024:(cc - 15) * 1024])
                     for cc in range(22)]
            w1a = [B.poolA[:, 0:1024], B.poolA[:, 1024:2048]]
            w1g = [B.poolA[:, 2048:3072], B.poolA[:, 3072:4096]]
            gel2 = [scb[:, 2560:3072], scb[:, 3072:3584]]
            gelf = [K.cnt["s_ve"]] * 2
            gj = [0]
            wtok = [("d_w0", 0), ("d_w1", 0)]
            wfree = [K.cnt["s_te"]] * 2
            banks = [B.PSA[:, 512 * i:512 * (i + 1)] for i in range(4)]
            bfree = [("s_ve", K.cnt["s_ve"])] * 4
            K.wait("te", "s_sc", K.cnt["s_sc"])
            bi = 0
            for fc in range(22):
                wb = fc % 2
                K.wait("gp", "s_te", wfree[wb])
                K.dma("gp", w1a[wb], t[f"f_w1a{li}"][:, fc * 1024:(fc + 1) * 1024],
                      sem=f"d_w{wb}")
                wtok[wb] = K.dma("gp", w1g[wb],
                                 t[f"f_w1g{li}"][:, fc * 1024:(fc + 1) * 1024],
                                 sem=f"d_w{wb}")
                for tt2 in range(2):
                    psa_ = banks[bi % 4]
                    psg_ = banks[(bi + 1) % 4]
                    K.wait("te", *wtok[wb])
                    K.wait("te", *bfree[bi % 4])
                    K.wait("te", *bfree[(bi + 1) % 4])
                    for kc in range(NG):
                        K.op("te", "matmul", psa_,
                             w1a[wb][:, kc * 128:(kc + 1) * 128],
                             xn[kc][:, tt2 * 512:(tt2 + 1) * 512],
                             start=(kc == 0), stop=(kc == NG - 1),
                             inc=("s_te" if kc == NG - 1 else None))
                    atok = K.cnt["s_te"]
                    for kc in range(NG):
                        K.op("te", "matmul", psg_,
                             w1g[wb][:, kc * 128:(kc + 1) * 128],
                             xn[kc][:, tt2 * 512:(tt2 + 1) * 512],
                             start=(kc == 0), stop=(kc == NG - 1),
                             inc=("s_te" if kc == NG - 1 else None))
                    gtok = K.cnt["s_te"]
                    if tt2 == 1:
                        wfree[wb] = gtok
                    j = gj[0] % 2
                    gj[0] += 1
                    gel = gel2[j]
                    stok = K.op("sc", "activation", gel, psg_, AF.Gelu,
                                bias=B.v_b1g[:, fc:fc + 1],
                                waits=[("s_te", gtok), ("s_ve", gelf[j])],
                                inc="s_sc")
                    vtok = K.op("ve", "scalar_tensor_tensor",
                                inter[fc][:, tt2 * 512:(tt2 + 1) * 512],
                                psa_, B.v_b1a[:, fc:fc + 1], gel,
                                ALU.add, ALU.mult,
                                waits=[("s_te", atok), ("s_sc", stok)],
                                inc="s_ve")
                    gelf[j] = vtok
                    bfree[bi % 4] = ("s_ve", vtok)
                    bfree[(bi + 1) % 4] = ("s_ve", vtok)
                    bi += 2
            K.barrier(("te", "ve", "sc", "gp"))
            K.set_scope(f"{base}.L{li}h{hf}.w2")
            bfree = [("s_ve", K.cnt["s_ve"])] * 4
            K.wait("te", *w2tok2)
            bi = 0
            for m in range(8):
                for tt2 in range(2):
                    ps = banks[bi % 4]
                    K.wait("te", *bfree[bi % 4])
                    for kc in range(22):
                        K.op("te", "matmul", ps, w2tile(m, kc),
                             inter[kc][:, tt2 * 512:(tt2 + 1) * 512],
                             start=(kc == 0), stop=(kc == 21),
                             inc=("s_te" if kc == 21 else None))
                    tetok = K.cnt["s_te"]
                    ysl = y[m][:, 1 + t0 + tt2 * 512:1 + t0 + (tt2 + 1) * 512]
                    vtok = K.op("ve", "scalar_tensor_tensor", ysl, ps,
                                B.v_b2[:, 8 * li + m:8 * li + m + 1], ysl,
                                ALU.add, ALU.add,
                                waits=[("s_te", tetok)], inc="s_ve")
                    bfree[bi % 4] = ("s_ve", vtok)
                    bi += 1
            K.barrier(("te", "ve", "sc", "gp"))
    K.barrier()
    K.set_scope(f"{base}.postln")
    emit_postln_residual(K, B, yv, xsrc, xdst, s)
    K.barrier()
    K.set_scope(base)


class RotarySteps:

    def __init__(self, K, B, ttok):
        self.K, self.B, self.ttok = K, B, ttok
        scb = B.scr.bitcast(BF16)
        self.tmp1 = scb[:, 0:512]
        self.tmp2 = scb[:, 512:1024]
        self.psR = B.PSC[:, 0:512]
        self.psBc = B.PST.bitcast(F32)[:, 0:512]
        self.sc_guard = K.cnt["s_sc"]
        self.rfree = ("s_ve", K.cnt["s_ve"])
        self.bfree = ("s_ve", K.cnt["s_ve"])
        self.todo = []

    def add(self, buf, t0, bc, bs, rows):
        for g in range(NG):
            for tt2 in range(2):
                self.todo.append((buf, t0, bc, bs, rows, g, tt2))

    def step(self, n=1):
        K, B = self.K, self.B
        for _ in range(n):
            if not self.todo:
                return
            buf, t0, bc, bs, rows, g, tt2 = self.todo.pop(0)
            sl = buf[g][:, t0 + tt2 * 512:t0 + (tt2 + 1) * 512]
            K.wait("te", "s_sc", self.sc_guard)
            K.wait("te", *self.rfree)
            rtok = K.op("te", "matmul", self.psR, B.rot, sl,
                        start=True, stop=True, waits=[self.ttok], inc="s_te")
            K.wait("te", *self.bfree)
            btok = K.op("te", "matmul", self.psBc,
                        B.exp16[:, 128 * g:128 * (g + 1)],
                        rows[:, tt2 * 512:(tt2 + 1) * 512],
                        start=True, stop=True, inc="s_te")
            K.op("ve", "tensor_tensor", self.tmp1, sl,
                 B.tabs[:, bc + tt2 * 512:bc + (tt2 + 1) * 512], ALU.mult,
                 waits=[("s_te", rtok)])
            K.op("ve", "tensor_tensor", self.tmp2, self.psR,
                 B.tabs[:, bs + tt2 * 512:bs + (tt2 + 1) * 512], ALU.mult,
                 waits=[("s_te", btok)])
            self.rfree = ("s_ve", K.cnt["s_ve"])
            K.op("ve", "tensor_add", self.tmp1, self.tmp1, self.tmp2)
            vtok = K.op("ve", "tensor_tensor", sl, self.tmp1, self.psBc,
                        ALU.mult, inc="s_ve")
            self.bfree = ("s_ve", vtok)


def _l0f_attention(K, B, qb, kb, vt):
    pfb = B.poolF.bitcast(BF16)
    C = B.col
    PSTs = [B.PST, B.PSB.bitcast(BF16)[:, 0:1024]]
    PSCs = [B.PSC[:, 0:256], B.PSB[:, 512:768]]
    units = [(g, h, w) for g in range(NG) for h in range(2) for w in range(8)]
    n = len(units)
    te0, ve0, sc0 = K.cnt["s_te"], K.cnt["s_ve"], K.cnt["s_sc"]
    stok = [[te0, te0] for _ in range(n)]
    mx_tok = [[ve0, ve0] for _ in range(n)]
    exp_tok = [sc0] * n
    norm_tok = [ve0] * n
    tr_tok = [te0] * n
    ev_tok = [sc0] * n
    pv_tok = [te0] * n
    out_tok = [ve0] * n

    def geom(u):
        g, h, w = units[u]
        if w > 0:
            subs = [(256 * (w - 1), 384, B.cmask),
                    (256 * (w - 1) + 128, 384, B.cmask)]
            kcs = [(0, 1, 2), (1, 2, 3)]
        else:
            subs = [(0, 128, B.w0m0), (0, 256, B.w0m1)]
            kcs = [(0,), (0, 1)]
        return g, h, 64 * h, w, subs, kcs

    def front_te(u):
        g, h, hb, w, subs, kcs = geom(u)
        if u >= 2:
            K.wait("te", "s_sc", exp_tok[u - 2])
        for sub, (koff, kwid, mask) in enumerate(subs):
            ps = B.PSA[:, (u % 2) * 1024 + sub * 512:
                       (u % 2) * 1024 + sub * 512 + kwid]
            tok = K.op("te", "matmul", ps,
                       qb[g][hb:hb + 64, w * 256 + sub * 128:
                             w * 256 + (sub + 1) * 128],
                       kb[g][hb:hb + 64, koff:koff + kwid],
                       start=True, stop=(sub == 0),
                       inc=("s_te" if sub == 0 else None))
            if sub == 1:
                tok = K.op("te", "matmul", ps, B.ident, mask,
                           start=False, stop=True, inc="s_te")
            stok[u][sub] = tok

    def mid(u):
        g, h, hb, w, subs, kcs = geom(u)
        cb = (u % 4) * 4
        slot = 1024 * (u % 4)
        K.op("ve", "reciprocal", C[:, cb + 2:cb + 4], C[:, cb + 2:cb + 4],
             waits=[("s_sc", exp_tok[u])])
        nm = 0
        for sub, (koff, kwid, mask) in enumerate(subs):
            Pp = pfb[:, slot + sub * 512:slot + sub * 512 + kwid]
            nm = K.op("ve", "tensor_scalar_mul", Pp, Pp,
                      C[:, cb + 2 + sub:cb + 3 + sub], inc="s_ve")
        cbase = [0, 128 * len(kcs[0])]
        colof = lambda sub, kc: cbase[sub] + 128 * (kc - kcs[sub][0])
        wd = 128 * (len(kcs[0]) + len(kcs[1]))
        pst = PSTs[u % 2]
        K.wait("te", "s_sc", ev_tok[u - 2] if u >= 2 else sc0)
        first = True
        for sub in range(2):
            for kc in kcs[sub]:
                koff = subs[sub][0]
                rel = kc - (koff - subs[0][0]) // 128 if w > 0 else kc
                K.op("te", "transpose",
                     pst[:, colof(sub, kc):colof(sub, kc) + 128],
                     pfb[:, slot + sub * 512 + 128 * rel:
                         slot + sub * 512 + 128 * (rel + 1)],
                     B.ident,
                     waits=[("s_ve", nm)] if first else [],
                     inc="s_te")
                first = False
        tr_tok[u] = K.cnt["s_te"]
        PT = pfb[:, 4096 + 1024 * (u % 2):4096 + 1024 * (u % 2) + 1024]
        w_ = [("s_te", tr_tok[u])]
        if u >= 2:
            w_.append(("s_te", pv_tok[u - 2]))
        evv = K.op("sc", "activation", PT[:, 0:wd], pst[:, 0:wd], AF.Copy,
                   waits=w_, inc="s_sc")
        ev_tok[u] = evv
        psc = PSCs[u % 2]
        c0 = 2 * (w - 1) if w > 0 else 0
        if u >= 2:
            K.wait("te", "s_ve", out_tok[u - 2])
        for sub in range(2):
            ks = kcs[sub]
            for i, kc in enumerate(ks):
                pv_tok[u] = K.op(
                    "te", "matmul", psc[:, sub * 128:(sub + 1) * 128],
                    vt[c0 + kc][:, g * 128:(g + 1) * 128],
                    PT[:, colof(sub, kc):colof(sub, kc) + 128],
                    start=(i == 0), stop=(i == len(ks) - 1),
                    waits=[("s_sc", evv)],
                    inc=("s_te" if i == len(ks) - 1 else None))

    def tail(u):
        g, h, hb, w, subs, kcs = geom(u)
        out_tok[u] = K.op("ve", "tensor_copy",
                          qb[g][hb:hb + 64, w * 256:(w + 1) * 256],
                          PSCs[u % 2][hb:hb + 64, 0:256],
                          waits=[("s_te", pv_tok[u])], inc="s_ve")

    def front_rest(u):
        g, h, hb, w, subs, kcs = geom(u)
        cb = (u % 4) * 4
        slot = 1024 * (u % 4)
        for sub, (koff, kwid, mask) in enumerate(subs):
            ps = B.PSA[:, (u % 2) * 1024 + sub * 512:
                       (u % 2) * 1024 + sub * 512 + kwid]
            if sub == 0:
                K.op("ve", "tensor_tensor", ps, ps, mask, ALU.add,
                     waits=[("s_te", stok[u][sub])])
                mx_tok[u][sub] = K.op("ve", "reduce_max",
                                      C[:, cb + sub:cb + sub + 1],
                                      ps, axis=AX.X, negate=True, inc="s_ve")
            else:
                mx_tok[u][sub] = K.op("ve", "reduce_max",
                                      C[:, cb + sub:cb + sub + 1],
                                      ps, axis=AX.X, negate=True,
                                      waits=[("s_te", stok[u][sub])],
                                      inc="s_ve")
            Pp = pfb[:, slot + sub * 512:slot + sub * 512 + kwid]
            w_ = [("s_ve", mx_tok[u][sub])]
            if sub == 0 and u >= 4:
                w_.append(("s_te", tr_tok[u - 4]))
            exp_tok[u] = K.op("sc", "activation", Pp, ps, AF.Exp,
                              bias=C[:, cb + sub:cb + sub + 1],
                              accum_out=C[:, cb + 2 + sub:cb + 3 + sub],
                              waits=w_, inc="s_sc")

    for u in range(n):
        front_te(u)
        if u >= 1:
            mid(u - 1)
        if u >= 2:
            tail(u - 2)
        front_rest(u)
    mid(n - 1)
    tail(n - 2)
    tail(n - 1)
